# version 16
# baseline (speedup 1.0000x reference)
import contextlib
import numpy as np
import ml_dtypes
import concourse.bass as bass
import concourse.mybir as mybir
from concourse.bass_utils import run_bass_kernel_spmd

F32 = mybir.dt.float32
BF16 = mybir.dt.bfloat16
AF = mybir.ActivationFunctionType
ALU = mybir.AluOpType
AX = mybir.AxisListType

L = 2048
D = 2048
DI = 4096
DS = 16
E = 8
FM = 1408
FS = 5632
CH = 512
NCH = L // CH
NT = CH // 128
KD = D // 128
KI = DI // 128
KF = FM // 128
KS = FS // 128
EPS = 1e-5

DEBUG = False


def _t(x):
    return np.ascontiguousarray(x)


def tile_lhsT(W, nk, nm, msz=128):
    WT = np.ascontiguousarray(np.asarray(W, np.float32).T)
    nin, nout = WT.shape
    assert nin == nk * 128 and nout == nm * msz, (W.shape, nk, nm, msz)
    return _t(WT.reshape(nk, 128, nm, msz).transpose(2, 1, 0, 3)
              .reshape(nm, 128, nk * msz))


def col128(v, nt):
    return _t(np.asarray(v, np.float32).reshape(nt, 128).T)


class Ctr:
    def __init__(self, h):
        self.h = h
        self.v = 0


def build_nc():
    nc = bass.Bass("TRN2", target_bir_lowering=False)

    def din(name, shape, dtype=F32):
        return nc.dram_tensor(name, shape, dtype, kind="ExternalInput").ap()

    def dout(name, shape, dtype=F32):
        return nc.dram_tensor(name, shape, dtype, kind="ExternalOutput").ap()

    def dint(name, shape, dtype=F32):
        kind = "ExternalOutput" if DEBUG else "Internal"
        return nc.dram_tensor(name, shape, dtype, kind=kind).ap()

    x_in = din("x", [L, D])
    w_in = din("w_in", [2 * KI, 128, KD * 128])
    w_xdt = din("w_xdt", [128, KI * 128])
    w_xB = din("w_xB", [128, KI * 16])
    w_xC = din("w_xC", [128, KI * 16])
    w_dtp = din("w_dtp", [KI, 128, 128])
    w_out_k = din("w_out_k", [KI, 2, 128, 8 * 128])
    w_gate = din("w_gate", [128, KD * 8])
    w_sgate = din("w_sgate", [128, KD])
    w_f1 = din("w_f1", [E, KF, 128, KD * 128])
    w_f2 = din("w_f2", [E, KF, 128, KD * 128])
    w_ep = din("w_ep", [E, KD, 128, KF * 128])
    w_s1 = din("w_s1", [KS, 128, KD * 128])
    w_s2 = din("w_s2", [KS, 128, KD * 128])
    w_sp = din("w_sp", [KD, 128, KS * 128])
    c_w = din("c_w", [128, KI * 4])
    c_b = din("c_b", [128, KI])
    dt_b = din("dt_b", [128, KI])
    a_log = din("a_log", [128, KI * DS])
    d_vec = din("d_vec", [128, KI])
    ident = din("ident", [128, 128], BF16)
    sel16 = din("sel16", [16, 16 * 128], BF16)
    sel8 = din("sel8", [8, 8 * 128], BF16)
    ones1 = din("ones1", [1, 128], BF16)

    out = dout("out", [L, D])
    xact_d = dint("xact_d", [KI, 128, L], BF16)
    z_d = dint("z_d", [KI, 128, L], BF16)
    yop_d = dint("yop_d", [KI, 128, L], BF16)
    x1_d = dint("x1_d", [L, D], F32)
    xn2_d = dint("xn2_d", [L // 128, KD, 128, 128], BF16)
    hsh_d = dint("hsh_d", [KS, 128, L], BF16)
    oac_d = dint("oac_d", [KD, 128, L], F32)
    dbg = {}
    if DEBUG:
        dbg["xn1"] = dout("dbg_xn1", [KD, 128, L], BF16)
        dbg["dt"] = dout("dbg_dt", [KI, 128, L], F32)
        dbg["ssm"] = dout("dbg_ssm", [KI, 128, L], F32)
        dbg["comb"] = dout("dbg_comb", [8, L], BF16)
        dbg["hbig1"] = dout("dbg_hbig1", [128, DS * CH], BF16)
        dbg["hcb1"] = dout("dbg_hcb1", [128, DS * CH], BF16)

    ctx = contextlib.ExitStack()
    with ctx:
        sb = ctx.enter_context
        gq = Ctr(sb(nc.semaphore("gq")))
        sq = Ctr(sb(nc.semaphore("sq")))
        cT = Ctr(sb(nc.semaphore("cT")))
        cV = Ctr(sb(nc.semaphore("cV")))
        cS = Ctr(sb(nc.semaphore("cS")))
        cP = Ctr(sb(nc.semaphore("cP")))
        SS = {}
        for _sn in ["xt", "wc", "xa0", "xa1", "z0", "z1", "dbg",
                    "xp0", "xp1", "xp2", "xc0", "xc1", "zb0", "zb1",
                    "yo0", "yo1", "yk0", "yk1", "yk2", "wk0", "wk1", "wk2",
                    "x1s", "x2s", "xt2", "wf0", "wf1", "wp0", "wp1",
                    "hs0", "hs1", "hb", "oa0", "oa1", "oa2", "oa3",
                    "oar", "outs", "x12", "wsm", "wg4", "glb", "xn2l"]:
            SS[_sn] = Ctr(sb(nc.semaphore("s_" + _sn)))
        block = ctx.enter_context(nc.Block())

        pend = {}

        def flush():
            for attr, fns in pend.items():
                if fns:
                    def body(e, fns=tuple(fns)):
                        for f in fns:
                            f(e)
                    getattr(block, attr)(body)
            pend.clear()

        def mk_emit(attr):
            def emit(op, *args, wait=(), inc=None, **kw):
                wl = []
                seen = set()
                for (c0, val) in wait:
                    if val > 0 and (id(c0), val) not in seen:
                        seen.add((id(c0), val))
                        wl.append((c0.h, val))
                if inc is not None:
                    c0, n = inc
                    ih, inn = c0.h, n
                    c0.v += n
                else:
                    ih = None
                    inn = 0

                def one(e, op=op, args=args, kw=kw, wl=tuple(wl), ih=ih,
                        inn=inn):
                    for (h, val) in wl:
                        e.wait_ge(h, val)
                    if op is not None:
                        inst = getattr(e, op)(*args, **kw)
                        if ih is not None:
                            inst.then_inc(ih, inn)
                pend.setdefault(attr, []).append(one)
            return emit

        T = mk_emit("tensor")
        V = mk_emit("vector")
        S = mk_emit("scalar")
        G = mk_emit("gpsimd")
        Y = mk_emit("sync")

        ident_b = sb(nc.sbuf_tensor("ident_b", [128, 128], BF16))
        ones1_b = sb(nc.sbuf_tensor("ones1_b", [1, 128], BF16))
        cw_sb = sb(nc.sbuf_tensor("cw_sb", [128, KI * 4], F32))
        cb_sb = sb(nc.sbuf_tensor("cb_sb", [128, KI], F32))
        dtb2_sb = sb(nc.sbuf_tensor("dtb2_sb", [128, KI], F32))
        A_sb = sb(nc.sbuf_tensor("A_sb", [128, KI * DS], F32))
        Dv_sb = sb(nc.sbuf_tensor("Dv_sb", [128, KI], F32))
        state_sb = sb(nc.sbuf_tensor("state_sb", [128, KI * DS], F32))
        stat_sb = sb(nc.sbuf_tensor("stat_sb", [128, 16], F32))
        lnsq = sb(nc.sbuf_tensor("lnsq", [128, D], F32))
        cst = sb(nc.sbuf_tensor("cst", [128, 2], F32))

        for dst, src in [(ident_b, ident), (ones1_b, ones1), (cw_sb, c_w),
                         (cb_sb, c_b), (dtb2_sb, dt_b), (A_sb, a_log),
                         (Dv_sb, d_vec)]:
            G("dma_start", out=dst[:], in_=src, inc=(SS["glb"], 16))
        glob_ld = SS["glb"].v
        S("activation", A_sb[:], A_sb[:], AF.Exp,
          wait=[(SS["glb"], glob_ld)], inc=(cS, 1))
        V("tensor_scalar", A_sb[:], A_sb[:], -1.0, None, ALU.mult,
          wait=[(cS, cS.v)], inc=(cV, 1))
        V("tensor_scalar", dtb2_sb[:], dtb2_sb[:], 2.0, None, ALU.mult,
          wait=[(SS["glb"], glob_ld)], inc=(cV, 1))
        V("memset", state_sb[:], 0.0, inc=(cV, 1))
        V("memset", cst[:, 0:1], EPS, inc=(cV, 1))
        V("memset", cst[:, 1:2], 1.0, inc=(cV, 1))
        ms_glob_v = cV.v
        ms_glob_s = cS.v

        fence_col = stat_sb

        def vfence():
            V("memset", fence_col[:, 15:16], 0.0, inc=(cV, 1))
            return cV.v

        def ln_tok(src_ap, dst_ap, dim):
            S("activation", lnsq[:, 0:dim], src_ap, AF.Square, inc=(cS, 1))
            sqv = cS.v
            V("reduce_sum", stat_sb[:, 0:1], src_ap, axis=AX.X, inc=(cV, 1))
            V("reduce_sum", stat_sb[:, 1:2], lnsq[:, 0:dim], axis=AX.X,
              wait=[(cS, sqv), (cV, cV.v)], inc=(cV, 1))
            V("tensor_scalar", stat_sb[:, 2:3], stat_sb[:, 0:1], 1.0 / dim,
              None, ALU.mult, wait=[(cV, cV.v)], inc=(cV, 1))
            V("tensor_tensor", stat_sb[:, 3:4], stat_sb[:, 2:3],
              stat_sb[:, 2:3], ALU.mult, wait=[(cV, cV.v)], inc=(cV, 1))
            V("scalar_tensor_tensor", stat_sb[:, 4:5], stat_sb[:, 1:2],
              1.0 / dim, stat_sb[:, 3:4], ALU.mult, ALU.subtract,
              wait=[(cV, cV.v)], inc=(cV, 1))
            S("activation", stat_sb[:, 5:6], stat_sb[:, 4:5], AF.Ln,
              bias=cst[:, 0:1], wait=[(cV, cV.v)], inc=(cS, 1))
            S("activation", stat_sb[:, 6:7], stat_sb[:, 5:6], AF.Exp,
              scale=-0.5, wait=[(cS, cS.v)], inc=(cS, 1))
            rstdv = cS.v
            V("tensor_scalar", stat_sb[:, 7:8], stat_sb[:, 2:3], -1.0, None,
              ALU.mult, inc=(cV, 1))
            V("tensor_scalar", dst_ap, src_ap, stat_sb[:, 7:8],
              stat_sb[:, 6:7], ALU.add, ALU.mult,
              wait=[(cS, rstdv), (cV, cV.v)], inc=(cV, 1))

        p01 = contextlib.ExitStack()
        en = p01.enter_context
        xtok = en(nc.sbuf_tensor("xtok", [128, 2 * D], F32))
        xn1tok = en(nc.sbuf_tensor("xn1tok", [128, D], BF16))
        xn1 = en(nc.sbuf_tensor("xn1", [128, KD * L], BF16))
        wcol = en(nc.sbuf_tensor("wcol", [128, 2 * KD * 128], BF16))
        xw = en(nc.sbuf_tensor("xw", [128, 2 * 520], F32))
        cacc = en(nc.sbuf_tensor("cacc", [128, 2 * CH], F32))
        xsil = en(nc.sbuf_tensor("xsil", [128, 2 * CH], BF16))
        zev = en(nc.sbuf_tensor("zev", [128, 2 * CH], BF16))
        ptr0 = en(nc.psum_tensor("ptr0", [128, 128], BF16))
        ptr1 = en(nc.psum_tensor("ptr1", [128, 128], BF16))
        pmm0 = en(nc.psum_tensor("pmm0", [128, 512], F32))
        pmm1 = en(nc.psum_tensor("pmm1", [128, 512], F32))

        ptr = [ptr0, ptr1]
        ptr_free = [0, 0]
        ntr = [0]

        def transpose_p1(dst_ap, src_ap, src_ms=0):
            b = ntr[0] % 2
            T("transpose", ptr[b][:], src_ap, ident_b[:],
              wait=[(cV, ptr_free[b]), (cV, src_ms), (SS["glb"], glob_ld)],
              inc=(cT, 1))
            V("tensor_copy", dst_ap, ptr[b][:], wait=[(cT, cT.v)],
              inc=(cV, 1))
            ptr_free[b] = cV.v
            ntr[0] += 1

        xn1tok_tg = 0
        xtok_free = [0, 0]
        for i in range(L // 128):
            b = i % 2
            Y("dma_start", out=xtok[:, b * D:(b + 1) * D],
              in_=x_in[i * 128:(i + 1) * 128, :],
              wait=[(cV, xtok_free[b])], inc=(SS["xt"], 16))
            V(None, wait=[(SS["xt"], SS["xt"].v), (cT, xn1tok_tg)])
            S(None, wait=[(SS["xt"], SS["xt"].v)])
            ln_tok(xtok[:, b * D:(b + 1) * D], xn1tok[:], D)
            xtok_free[b] = cV.v
            xn1tok_ms = cV.v
            for ct in range(KD):
                transpose_p1(
                    xn1[:, ct * L + i * 128: ct * L + (i + 1) * 128],
                    xn1tok[:, ct * 128:(ct + 1) * 128], src_ms=xn1tok_ms)
            xn1tok_tg = cT.v
        ms_xn1 = cV.v
        if DEBUG:
            Y("dma_start", out=dbg["xn1"].transpose([1, 0, 2]),
              in_=xn1.ap().rearrange("p (c l) -> p c l", l=L),
              wait=[(cV, ms_xn1)], inc=(SS["dbg"], 16))

        pmm = [pmm0, pmm1]
        pmm_free = [0, 0]
        wcol_free = [0, 0]
        ms_xact = {}
        ms_z = {}
        conv_guard = 0
        xsil_store = [0, 0]
        zev_store = [0, 0]
        for m in range(2 * KI):
            b = m % 2
            G("dma_start", out=wcol[:, b * KD * 128:(b + 1) * KD * 128],
              in_=w_in[m], wait=[(cT, wcol_free[b])], inc=(SS["wc"], 16))
            ms_wm = SS["wc"].v
            is_x = m < KI
            d = m if is_x else m - KI
            if is_x:
                wb = d % 2
                V("memset", xw[:, wb * 520: wb * 520 + 4], 0.0, inc=(cV, 1))
            for c in range(NCH):
                pb = (m * NCH + c) % 2
                for k in range(KD):
                    w0 = ([(SS["wc"], ms_wm), (cV, ms_xn1),
                           (cV, pmm_free[pb])] if k == 0 else ())
                    T("matmul", pmm[pb][:],
                      wcol[:, b * KD * 128 + k * 128:
                           b * KD * 128 + (k + 1) * 128],
                      xn1[:, k * L + c * CH: k * L + (c + 1) * CH],
                      start=(k == 0), stop=(k == KD - 1), wait=w0,
                      inc=((cT, 1) if k == KD - 1 else None))
                mm_done = cT.v
                if c == NCH - 1:
                    wcol_free[b] = mm_done
                if is_x:
                    V("tensor_copy", xw[:, wb * 520 + 4: wb * 520 + 4 + CH],
                      pmm[pb][:], wait=[(cT, mm_done)], inc=(cV, 1))
                    pmm_free[pb] = cV.v
                    ca0 = cacc[:, 0:CH]
                    ca1 = cacc[:, CH:2 * CH]
                    V("tensor_scalar", ca0,
                      xw[:, wb * 520 + 1: wb * 520 + 1 + CH],
                      cw_sb[:, d * 4: d * 4 + 1], None, ALU.mult,
                      wait=[(cV, cV.v)], inc=(cV, 1))
                    V("scalar_tensor_tensor", ca1,
                      xw[:, wb * 520 + 2: wb * 520 + 2 + CH],
                      cw_sb[:, d * 4 + 1: d * 4 + 2], ca0,
                      ALU.mult, ALU.add, wait=[(cS, conv_guard)],
                      inc=(cV, 1))
                    V("scalar_tensor_tensor", ca0,
                      xw[:, wb * 520 + 3: wb * 520 + 3 + CH],
                      cw_sb[:, d * 4 + 2: d * 4 + 3], ca1,
                      ALU.mult, ALU.add, inc=(cV, 1))
                    V("scalar_tensor_tensor", ca1,
                      xw[:, wb * 520 + 4: wb * 520 + 4 + CH],
                      cw_sb[:, d * 4 + 3: d * 4 + 4], ca0,
                      ALU.mult, ALU.add, inc=(cV, 1))
                    conv_done = cV.v
                    V("tensor_copy", xw[:, wb * 520: wb * 520 + 4],
                      xw[:, wb * 520 + CH: wb * 520 + CH + 4], inc=(cV, 1))
                    spp = (m * NCH + c) % 2
                    sxa = SS["xa0"] if spp == 0 else SS["xa1"]
                    S("activation", xsil[:, spp * CH:(spp + 1) * CH], ca1,
                      AF.Silu, bias=cb_sb[:, d:d + 1],
                      wait=[(cV, conv_done), (sxa, xsil_store[spp])],
                      inc=(cS, 1))
                    conv_guard = cS.v
                    Y("dma_start", out=xact_d[d, :, c * CH:(c + 1) * CH],
                      in_=xsil[:, spp * CH:(spp + 1) * CH],
                      wait=[(cS, cS.v)], inc=(sxa, 16))
                    xsil_store[spp] = sxa.v
                else:
                    zp = (m * NCH + c) % 2
                    szv = SS["z0"] if zp == 0 else SS["z1"]
                    V("tensor_copy", zev[:, zp * CH:(zp + 1) * CH],
                      pmm[pb][:],
                      wait=[(cT, mm_done), (szv, zev_store[zp])],
                      inc=(cV, 1))
                    pmm_free[pb] = cV.v
                    Y("dma_start", out=z_d[d, :, c * CH:(c + 1) * CH],
                      in_=zev[:, zp * CH:(zp + 1) * CH],
                      wait=[(cV, vfence())], inc=(szv, 16))
                    zev_store[zp] = szv.v
        flush()
        p01.close()

        p2 = contextlib.ExitStack()
        en = p2.enter_context
        wxdt_sb = en(nc.sbuf_tensor("wxdt_sb", [128, KI * 128], BF16))
        wxB_sb = en(nc.sbuf_tensor("wxB_sb", [128, KI * 16], BF16))
        wxC_sb = en(nc.sbuf_tensor("wxC_sb", [128, KI * 16], BF16))
        wdtp_sb = en(nc.sbuf_tensor("wdtp_sb", [128, KI * 128], BF16))
        sel16_b = en(nc.sbuf_tensor("sel16_b", [16, 16 * 128], BF16))
        xps = en(nc.sbuf_tensor("xps", [128, 3 * CH], BF16))
        xsc = en(nc.sbuf_tensor("xsc", [128, 2 * CH], BF16))
        dtlow = en(nc.sbuf_tensor("dtlow", [128, CH], BF16))
        bcsb = en(nc.sbuf_tensor("bcsb", [16, 2 * CH], BF16))
        Bp = en(nc.sbuf_tensor("Bp", [128, DS * CH], BF16))
        Cp = en(nc.sbuf_tensor("Cp", [128, DS * CH], BF16))
        spu = en(nc.sbuf_tensor("spu", [128, 2 * CH], F32))
        spt = en(nc.sbuf_tensor("spt", [128, 2 * CH], F32))
        dtf = en(nc.sbuf_tensor("dtf", [128, 2 * CH], F32))
        u2 = en(nc.sbuf_tensor("u2", [128, 2 * CH], BF16))
        da = en(nc.sbuf_tensor("da", [128, 2 * CH], F32))
        db = en(nc.sbuf_tensor("db", [128, 3 * CH], BF16))
        hbig = en(nc.sbuf_tensor("hbig", [128, DS * CH], BF16))
        hcb = en(nc.sbuf_tensor("hcb", [128, DS * CH], BF16))
        ysum = en(nc.sbuf_tensor("ysum", [128, CH], F32))
        yt = en(nc.sbuf_tensor("yt", [128, CH], F32))
        zbb = en(nc.sbuf_tensor("zbb", [128, 2 * CH], BF16))
        yob = en(nc.sbuf_tensor("yob", [128, 2 * CH], BF16))
        wko = en(nc.sbuf_tensor("wko", [128, 3 * 8 * 128], BF16))
        yks = en(nc.sbuf_tensor("yks", [128, 3 * CH], BF16))
        moev = en(nc.sbuf_tensor("moev", [128, KD * CH], BF16))
        motok = en(nc.sbuf_tensor("motok", [128, NT * D], BF16))
        xtok2 = en(nc.sbuf_tensor("xtok2", [128, D], F32))
        x1tok = en(nc.sbuf_tensor("x1tok", [128, D], F32))
        xn2tok = en(nc.sbuf_tensor("xn2tok", [128, D], BF16))
        stage16 = en(nc.sbuf_tensor("stage16", [128, KD * 128], BF16))
        tr1 = motok.ap()[:, 0:8 * CH]
        tr2 = motok.ap()[:, 8 * CH:12 * CH]

        G("dma_start", out=wxdt_sb[:], in_=w_xdt, inc=(SS["wsm"], 16))
        G("dma_start", out=wxB_sb[:], in_=w_xB, inc=(SS["wsm"], 16))
        G("dma_start", out=wxC_sb[:], in_=w_xC, inc=(SS["wsm"], 16))
        G("dma_start",
          out=wdtp_sb.ap().rearrange("b (a c) -> b a c", c=128),
          in_=w_dtp.transpose([1, 0, 2]), inc=(SS["wsm"], 16))
        G("dma_start", out=sel16_b[:], in_=sel16, inc=(SS["wsm"], 16))
        ms_wsm = SS["wsm"].v

        ms_yop = {}
        ms_x1row = {}
        dt_guard = [0, 0]
        sp_guard = [0, 0]
        sp_vguard = [0, 0]
        da_guard = [0, 0]
        zb_guard = [0, 0]
        zb_ld = [0, 0]
        yo_store = [0, 0]
        xsc_guard = [0, 0]
        xsc_ld = [0, 0]
        motok_sguard = 0
        hcb_vguard = 0
        tr_vguard = 0
        pool_s_ms = [0] * DS
        x1_sguard = 0
        xt2_vguard = 0
        xn2tok_tg = 0
        st16_sq = 0
        wko_guard = 0
        last_xn2_store = 0

        for c in range(NCH):
            psA = contextlib.ExitStack()
            pdt = psA.enter_context(nc.psum_tensor(f"pdt{c}", [128, 512], F32))
            pB = psA.enter_context(nc.psum_tensor(f"pB{c}", [16, 512], F32))
            pC = psA.enter_context(nc.psum_tensor(f"pC{c}", [16, 512], F32))
            pbc = psA.enter_context(nc.psum_tensor(f"pbc{c}", [128, 512], F32))
            T(None, wait=[(cV, cV.v), (cS, cS.v)])
            xps_ld = [0, 0, 0]
            xps_guard = [0, 0, 0]
            xpss = [SS["xp0"], SS["xp1"], SS["xp2"]]
            for k in range(3):
                Y("dma_start", out=xps[:, k * CH:(k + 1) * CH],
                  in_=xact_d[k, :, c * CH:(c + 1) * CH],
                  wait=[(SS["xa0"], SS["xa0"].v), (SS["xa1"], SS["xa1"].v)],
                  inc=(xpss[k], 16))
                xps_ld[k] = xpss[k].v
            for k in range(KI):
                sl = k % 3
                w0 = [(xpss[sl], xps_ld[sl]), (SS["wsm"], ms_wsm)]
                T("matmul", pdt[:], wxdt_sb[:, k * 128:(k + 1) * 128],
                  xps[:, sl * CH:(sl + 1) * CH], start=(k == 0),
                  stop=(k == KI - 1), wait=w0,
                  inc=((cT, 1) if k == KI - 1 else None))
                T("matmul", pB[:], wxB_sb[:, k * 16:(k + 1) * 16],
                  xps[:, sl * CH:(sl + 1) * CH], start=(k == 0),
                  stop=(k == KI - 1),
                  inc=((cT, 1) if k == KI - 1 else None))
                T("matmul", pC[:], wxC_sb[:, k * 16:(k + 1) * 16],
                  xps[:, sl * CH:(sl + 1) * CH], start=(k == 0),
                  stop=(k == KI - 1), inc=(cT, 1))
                xps_guard[sl] = cT.v
                if k + 3 < KI:
                    Y("dma_start", out=xps[:, sl * CH:(sl + 1) * CH],
                      in_=xact_d[k + 3, :, c * CH:(c + 1) * CH],
                      wait=[(cT, xps_guard[sl])], inc=(xpss[sl], 16))
                    xps_ld[sl] = xpss[sl].v
            ms_xproj = cT.v
            V("tensor_copy", dtlow[:], pdt[:], wait=[(cT, ms_xproj)],
              inc=(cV, 1))
            ms_dtlow = cV.v
            V("tensor_copy", bcsb[:, 0:CH], pB[:], wait=[(cT, ms_xproj)],
              inc=(cV, 1))
            V("tensor_copy", bcsb[:, CH:2 * CH], pC[:],
              wait=[(cT, ms_xproj)], inc=(cV, 1))
            ms_bc = cV.v
            for s in range(DS):
                T("matmul", pbc[:], sel16_b[:, s * 128:(s + 1) * 128],
                  bcsb[:, 0:CH], start=True, stop=True,
                  wait=[(cV, ms_bc), (cV, cV.v)], inc=(cT, 1))
                V("tensor_copy", Bp[:, s * CH:(s + 1) * CH], pbc[:],
                  wait=[(cT, cT.v)], inc=(cV, 1))
                T("matmul", pbc[:], sel16_b[:, s * 128:(s + 1) * 128],
                  bcsb[:, CH:2 * CH], start=True, stop=True,
                  wait=[(cV, cV.v)], inc=(cT, 1))
                V("tensor_copy", Cp[:, s * CH:(s + 1) * CH], pbc[:],
                  wait=[(cT, cT.v)], inc=(cV, 1))
            ms_bcp = cV.v
            flush()
            psA.close()

            psB = contextlib.ExitStack()
            pdd = psB.enter_context(nc.psum_tensor(f"pdd{c}", [128, 512], F32))
            T(None, wait=[(cV, cV.v), (cS, cS.v)])
            pdd_free = 0
            for d in range(KI):
                wb = d % 2
                T("matmul", pdd[:], wdtp_sb[:, d * 128:(d + 1) * 128],
                  dtlow[:], start=True, stop=True,
                  wait=[(cV, pdd_free), (cV, ms_dtlow), (SS["wsm"], ms_wsm)],
                  inc=(cT, 1))
                ms_dd = cT.v
                uo = wb * CH
                V("tensor_scalar", spu[:, uo:uo + CH], pdd[:],
                  dtb2_sb[:, d:d + 1], None, ALU.add,
                  wait=[(cT, ms_dd), (cS, sp_guard[wb]), (cV, ms_glob_v)],
                  inc=(cV, 1))
                ms_u = cV.v
                pdd_free = cV.v
                S("activation", spt[:, uo:uo + CH], spu[:, uo:uo + CH],
                  AF.Abs, wait=[(cV, ms_u), (cV, sp_vguard[wb])],
                  inc=(cS, 1))
                sp_guard[wb] = cS.v
                S("activation", spt[:, uo:uo + CH], spt[:, uo:uo + CH],
                  AF.Exp, scale=-1.0, wait=[(cS, cS.v)], inc=(cS, 1))
                S("activation", spt[:, uo:uo + CH], spt[:, uo:uo + CH],
                  AF.Ln, bias=cst[:, 1:2], wait=[(cS, cS.v)], inc=(cS, 1))
                ms_lnx = cS.v
                V("tensor_scalar", spu[:, uo:uo + CH], spu[:, uo:uo + CH],
                  0.0, None, ALU.max, wait=[(cS, sp_guard[wb])],
                  inc=(cV, 1))
                V("tensor_tensor", dtf[:, uo:uo + CH], spu[:, uo:uo + CH],
                  spt[:, uo:uo + CH], ALU.add,
                  wait=[(cS, max(ms_lnx, dt_guard[wb]))], inc=(cV, 1))
                ms_dt = cV.v
                sp_vguard[wb] = cV.v
                if DEBUG:
                    Y("dma_start", out=dbg["dt"][d, :, c * CH:(c + 1) * CH],
                      in_=dtf[:, uo:uo + CH], wait=[(cV, ms_dt)],
                      inc=(SS["dbg"], 16))
                    V(None, wait=[(SS["dbg"], SS["dbg"].v)])
                xp2 = d % 2
                sxc = SS["xc0"] if xp2 == 0 else SS["xc1"]
                Y("dma_start", out=xsc[:, xp2 * CH:(xp2 + 1) * CH],
                  in_=xact_d[d, :, c * CH:(c + 1) * CH],
                  wait=[(SS["xa0"], SS["xa0"].v), (SS["xa1"], SS["xa1"].v),
                        (cV, xsc_guard[xp2])],
                  inc=(sxc, 16))
                xsc_ld[xp2] = sxc.v
                V("tensor_tensor", u2[:, uo:uo + CH], dtf[:, uo:uo + CH],
                  xsc[:, xp2 * CH:(xp2 + 1) * CH], ALU.mult,
                  wait=[(sxc, xsc_ld[xp2])], inc=(cV, 1))
                ms_u2 = cV.v
                zp = d % 2
                szb = SS["zb0"] if zp == 0 else SS["zb1"]
                Y("dma_start", out=zbb[:, zp * CH:(zp + 1) * CH],
                  in_=z_d[d, :, c * CH:(c + 1) * CH],
                  wait=[(SS["z0"], SS["z0"].v), (SS["z1"], SS["z1"].v),
                        (cV, zb_guard[zp])],
                  inc=(szb, 16))
                zb_ld[zp] = szb.v

                da_ms = [0] * DS
                for s in range(DS + 2):
                    if s < DS:
                        ab = s % 2
                        S("activation", da[:, ab * CH:(ab + 1) * CH],
                          dtf[:, uo:uo + CH], AF.Exp,
                          scale=A_sb[:, d * DS + s: d * DS + s + 1],
                          wait=[(cV, max(ms_dt, da_guard[ab])),
                                (cS, ms_glob_s)], inc=(cS, 1))
                        da_ms[s] = cS.v
                        sl3 = s % 3
                        V("tensor_tensor", db[:, sl3 * CH:(sl3 + 1) * CH],
                          u2[:, uo:uo + CH], Bp[:, s * CH:(s + 1) * CH],
                          ALU.mult, wait=[(cV, max(ms_u2, ms_bcp))],
                          inc=(cV, 1))
                    if 1 <= s <= DS:
                        s1_ = s - 1
                        ab = s1_ % 2
                        V("tensor_tensor_scan",
                          hbig[:, s1_ * CH:(s1_ + 1) * CH],
                          da[:, ab * CH:(ab + 1) * CH],
                          db[:, (s1_ % 3) * CH:((s1_ % 3) + 1) * CH],
                          state_sb[:, d * DS + s1_: d * DS + s1_ + 1],
                          ALU.mult, ALU.add,
                          wait=[(cS, da_ms[s1_])], inc=(cV, 1))
                        da_guard[ab] = cV.v
                    if s >= 2:
                        s2_ = s - 2
                        V("tensor_tensor", hcb[:, s2_ * CH:(s2_ + 1) * CH],
                          hbig[:, s2_ * CH:(s2_ + 1) * CH],
                          Cp[:, s2_ * CH:(s2_ + 1) * CH], ALU.mult,
                          inc=(cV, 1))
                if DEBUG and c == 1 and d == 0:
                    V(None, wait=[(cP, cP.v)])
                    Y("dma_start", out=dbg["hbig1"], in_=hbig[:],
                      wait=[(cV, cV.v)], inc=(SS["dbg"], 16))
                    Y("dma_start", out=dbg["hcb1"], in_=hcb[:],
                      wait=[(cV, cV.v)], inc=(SS["dbg"], 16))
                    V(None, wait=[(SS["dbg"], SS["dbg"].v)])
                V("tensor_copy", state_sb[:, d * DS:(d + 1) * DS],
                  hbig[:, CH - 1: DS * CH: CH], inc=(cV, 1))
                V("tensor_tensor", tr1, hcb[:, 0:8 * CH],
                  hcb[:, 8 * CH:16 * CH], ALU.add,
                  wait=[(cS, motok_sguard)], inc=(cV, 1))
                hcb_vguard = cV.v
                V("tensor_tensor", tr2, tr1[:, 0:4 * CH],
                  tr1[:, 4 * CH:8 * CH], ALU.add, wait=[(cV, cV.v)],
                  inc=(cV, 1))
                V("tensor_tensor", tr1[:, 0:2 * CH], tr2[:, 0:2 * CH],
                  tr2[:, 2 * CH:4 * CH], ALU.add, wait=[(cV, cV.v)],
                  inc=(cV, 1))
                V("tensor_tensor", ysum[:], tr1[:, 0:CH],
                  tr1[:, CH:2 * CH], ALU.add, wait=[(cV, cV.v)],
                  inc=(cV, 1))
                if DEBUG:
                    Y("dma_start", out=dbg["ssm"][d, :, c * CH:(c + 1) * CH],
                      in_=ysum[:], wait=[(cV, cV.v)], inc=(SS["dbg"], 16))
                    V(None, wait=[(SS["dbg"], SS["dbg"].v)])
                V("scalar_tensor_tensor", yt[:],
                  xsc[:, xp2 * CH:(xp2 + 1) * CH], Dv_sb[:, d:d + 1],
                  ysum[:], ALU.mult, ALU.add, wait=[(cV, cV.v)],
                  inc=(cV, 1))
                xsc_guard[xp2] = cV.v
                yp = d % 2
                syo = SS["yo0"] if yp == 0 else SS["yo1"]
                V("tensor_tensor", yob[:, yp * CH:(yp + 1) * CH], yt[:],
                  zbb[:, zp * CH:(zp + 1) * CH], ALU.mult,
                  wait=[(szb, zb_ld[zp]), (syo, yo_store[yp]),
                        (cV, cV.v)], inc=(cV, 1))
                zb_guard[zp] = cV.v
                dt_guard[wb] = cS.v
                Y("dma_start", out=yop_d[d, :, c * CH:(c + 1) * CH],
                  in_=yob[:, yp * CH:(yp + 1) * CH],
                  wait=[(cV, vfence())], inc=(syo, 16))
                yo_store[yp] = syo.v
            flush()
            psB.close()

            psC = contextlib.ExitStack()
            po = [psC.enter_context(
                nc.psum_tensor(f"po{c}_{j}", [128, 512], F32))
                for j in range(8)]
            T(None, wait=[(cV, cV.v), (cS, cS.v)])
            wkss = [SS["wk0"], SS["wk1"], SS["wk2"]]
            ykss = [SS["yk0"], SS["yk1"], SS["yk2"]]
            for half in range(2):
                wk_ld = [0, 0, 0]
                wk_guard = [0, 0, 0]
                yk_ld = [0, 0, 0]
                for k in range(3):
                    G("dma_start", out=wko[:, k * 1024:(k + 1) * 1024],
                      in_=w_out_k[k, half], wait=[(cT, wko_guard)],
                      inc=(wkss[k], 16))
                    wk_ld[k] = wkss[k].v
                    Y("dma_start", out=yks[:, k * CH:(k + 1) * CH],
                      in_=yop_d[k, :, c * CH:(c + 1) * CH],
                      wait=[(SS["yo0"], SS["yo0"].v),
                            (SS["yo1"], SS["yo1"].v), (cT, wko_guard)],
                      inc=(ykss[k], 16))
                    yk_ld[k] = ykss[k].v
                for k in range(KI):
                    sl = k % 3
                    for j in range(8):
                        w0 = ([(wkss[sl], wk_ld[sl]), (ykss[sl], yk_ld[sl])]
                              if j == 0 else ())
                        if k == 0 and j == 0 and half == 1:
                            w0 = w0 + [(cV, cV.v)]
                        T("matmul", po[j][:],
                          wko[:, sl * 1024 + j * 128:
                               sl * 1024 + (j + 1) * 128],
                          yks[:, sl * CH:(sl + 1) * CH], start=(k == 0),
                          stop=(k == KI - 1), wait=w0,
                          inc=((cT, 1) if j == 7 else None))
                    wk_guard[sl] = cT.v
                    if k + 3 < KI:
                        G("dma_start",
                          out=wko[:, sl * 1024:(sl + 1) * 1024],
                          in_=w_out_k[k + 3, half],
                          wait=[(cT, wk_guard[sl])], inc=(wkss[sl], 16))
                        wk_ld[sl] = wkss[sl].v
                        Y("dma_start", out=yks[:, sl * CH:(sl + 1) * CH],
                          in_=yop_d[k + 3, :, c * CH:(c + 1) * CH],
                          wait=[(cT, wk_guard[sl])], inc=(ykss[sl], 16))
                        yk_ld[sl] = ykss[sl].v
                ms_oph = cT.v
                wko_guard = cT.v
                for j in range(8):
                    V("tensor_copy",
                      moev[:, (half * 8 + j) * CH:(half * 8 + j + 1) * CH],
                      po[j][:], wait=[(cT, ms_oph)], inc=(cV, 1))
            ms_moev = cV.v
            flush()
            psC.close()

            psD = contextlib.ExitStack()
            ptA = psD.enter_context(nc.psum_tensor(f"ptA{c}", [128, 128], BF16))
            ptB = psD.enter_context(nc.psum_tensor(f"ptB{c}", [128, 128], BF16))
            pt2 = [ptA, ptB]
            pt2_free = [0, 0]
            nt2 = [0]
            T(None, wait=[(cV, cV.v), (cS, cS.v)])

            def transpose_p2(dst_ap, src_ap, src_ms):
                b2 = nt2[0] % 2
                T("transpose", pt2[b2][:], src_ap, ident_b[:],
                  wait=[(cV, pt2_free[b2]), (cV, src_ms)], inc=(cT, 1))
                V("tensor_copy", dst_ap, pt2[b2][:], wait=[(cT, cT.v)],
                  inc=(cV, 1))
                pt2_free[b2] = cV.v
                nt2[0] += 1

            V(None, wait=[(cS, motok_sguard)])
            for mg in range(KD):
                for tt in range(NT):
                    transpose_p2(
                        motok[:, tt * D + mg * 128: tt * D + (mg + 1) * 128],
                        moev[:, mg * CH + tt * 128: mg * CH + (tt + 1) * 128],
                        ms_moev)
            ms_motok = cV.v

            for tt in range(NT):
                ti = c * NT + tt
                Y("dma_start", out=xtok2[:],
                  in_=x_in[ti * 128:(ti + 1) * 128, :],
                  wait=[(cV, xt2_vguard)], inc=(SS["xt2"], 16))
                ms_xt2 = SS["xt2"].v
                V(None, wait=[(cV, ms_motok)])
                S(None, wait=[(cV, ms_motok), (cS, motok_sguard)])
                ln_tok(motok[:, tt * D:(tt + 1) * D], lnsq[:, 0:D], D)
                motok_sguard = cS.v
                V("tensor_tensor", x1tok[:], lnsq[:, 0:D], xtok2[:],
                  ALU.add, wait=[(SS["xt2"], ms_xt2),
                                 (SS["x1s"], SS["x1s"].v),
                                 (cS, x1_sguard), (cV, cV.v)], inc=(cV, 1))
                xt2_vguard = cV.v
                Y("dma_start", out=x1_d[ti * 128:(ti + 1) * 128, :],
                  in_=x1tok[:], wait=[(cV, vfence())], inc=(SS["x1s"], 16))
                V(None, wait=[(cV, cV.v), (cT, xn2tok_tg)])
                S(None, wait=[(cV, cV.v)])
                ln_tok(x1tok[:], xn2tok[:], D)
                x1_sguard = cS.v
                xn2tok_ms = cV.v
                V(None, wait=[(SS["x2s"], SS["x2s"].v)])
                for ct in range(KD):
                    transpose_p2(stage16[:, ct * 128:(ct + 1) * 128],
                                 xn2tok[:, ct * 128:(ct + 1) * 128],
                                 xn2tok_ms)
                xn2tok_tg = cT.v
                Y("dma_start", out=xn2_d[ti].transpose([1, 0, 2]),
                  in_=stage16.ap().rearrange("p (c t) -> p c t", t=128),
                  wait=[(cV, vfence())], inc=(SS["x2s"], 16))
            flush()
            psD.close()
        flush()
        p2.close()

        p4 = contextlib.ExitStack()
        en = p4.enter_context
        xn2 = en(nc.sbuf_tensor("xn2", [128, KD * L], BF16))
        sel8_b = en(nc.sbuf_tensor("sel8_b", [8, 8 * 128], BF16))
        wgate_sb = en(nc.sbuf_tensor("wgate_sb", [128, KD * 8], BF16))
        wsgate_sb = en(nc.sbuf_tensor("wsgate_sb", [128, KD], BF16))
        combr = en(nc.sbuf_tensor("combr", [8, L], BF16))
        wfa = en(nc.sbuf_tensor("wfa", [128, 2 * KD * 128], BF16))
        wfb = en(nc.sbuf_tensor("wfb", [128, 2 * KD * 128], BF16))
        s1b = en(nc.sbuf_tensor("s1b", [128, 2 * CH], BF16))
        gt = en(nc.sbuf_tensor("gt", [128, CH], BF16))
        hbk = en(nc.sbuf_tensor("hbk", [128, KF * L], BF16))
        wpc = en(nc.sbuf_tensor("wpc", [128, 3 * KF * 128], BF16))
        oacst = en(nc.sbuf_tensor("oacst", [128, 2 * L], F32))
        fotok = en(nc.sbuf_tensor("fotok", [128, D], BF16))
        x1t2 = en(nc.sbuf_tensor("x1t2", [128, D], F32))
        otok = en(nc.sbuf_tensor("otok", [128, D], F32))
        oacr = en(nc.sbuf_tensor("oacr", [128, 2 * KD * 128], BF16))
        pbc2 = en(nc.psum_tensor("pbcg", [128, 512], F32))

        G("dma_start", out=wgate_sb[:], in_=w_gate, inc=(SS["wg4"], 16))
        G("dma_start", out=wsgate_sb[:], in_=w_sgate, inc=(SS["wg4"], 16))
        G("dma_start", out=sel8_b[:], in_=sel8, inc=(SS["wg4"], 16))
        ms_wg = SS["wg4"].v
        for ct in range(KD):
            Y("dma_start",
              out=xn2.ap()[:, ct * L:(ct + 1) * L]
              .rearrange("p (i t) -> p i t", t=128),
              in_=xn2_d[:, ct].transpose([1, 0, 2]),
              wait=[(SS["x2s"], SS["x2s"].v)], inc=(SS["xn2l"], 16))
        ms_xn2r = SS["xn2l"].v

        p4_free = [0, 0]
        ntr4 = [0]
        cpad_tg = 0

        psG = contextlib.ExitStack()
        gbuf = psG.enter_context(nc.sbuf_tensor("gbuf", [128, L], BF16))
        psR = contextlib.ExitStack()
        lgpad = psR.enter_context(nc.sbuf_tensor("lgpad", [128, 128], BF16))
        lgt = psR.enter_context(nc.sbuf_tensor("lgt", [128, 8], F32))
        smx = psR.enter_context(nc.sbuf_tensor("smx", [128, 48], F32))
        cpad = psR.enter_context(nc.sbuf_tensor("cpad", [128, 128], BF16))
        grow = psR.enter_context(nc.sbuf_tensor("grow", [1, CH], BF16))
        pR = psR.enter_context(nc.psum_tensor("pR", [8, 512], F32))
        pG = psR.enter_context(nc.psum_tensor("pG", [1, 512], F32))
        ptrC = psR.enter_context(nc.psum_tensor("ptrC", [128, 128], BF16))
        ptrD = psR.enter_context(nc.psum_tensor("ptrD", [128, 128], BF16))
        ptr4 = [ptrC, ptrD]
        V("memset", lgpad[:], 0.0, inc=(cV, 1))
        V("memset", cpad[:], 0.0, inc=(cV, 1))
        pR_free = pG_free = pbc2_free = 0
        lgpad_guard = 0
        for c in range(NCH):
            for k in range(KD):
                w0 = ([(SS["wg4"], ms_wg), (SS["xn2l"], ms_xn2r), (cV, pR_free),
                       (cV, pG_free)] if k == 0 else ())
                T("matmul", pR[:], wgate_sb[:, k * 8:(k + 1) * 8],
                  xn2[:, k * L + c * CH: k * L + (c + 1) * CH],
                  start=(k == 0), stop=(k == KD - 1), wait=w0,
                  inc=((cT, 1) if k == KD - 1 else None))
                T("matmul", pG[:], wsgate_sb[:, k:k + 1],
                  xn2[:, k * L + c * CH: k * L + (c + 1) * CH],
                  start=(k == 0), stop=(k == KD - 1),
                  inc=((cT, 1) if k == KD - 1 else None))
            ms_rg = cT.v
            V("tensor_copy", grow[:], pG[:], wait=[(cT, ms_rg)], inc=(cV, 1))
            pG_free = cV.v
            S("activation", grow[:], grow[:], AF.Sigmoid,
              wait=[(cV, cV.v)], inc=(cS, 1))
            T("matmul", pbc2[:], ones1_b[:], grow[:], start=True, stop=True,
              wait=[(cS, cS.v), (cV, pbc2_free)], inc=(cT, 1))
            V("tensor_copy", gbuf[:, c * CH:(c + 1) * CH], pbc2[:],
              wait=[(cT, cT.v)], inc=(cV, 1))
            pbc2_free = cV.v
            for tt in range(NT):
                V("tensor_copy", lgpad[0:8, :],
                  pR[:, tt * 128:(tt + 1) * 128],
                  wait=[(cT, ms_rg), (cT, lgpad_guard), (cV, ms_glob_v)],
                  inc=(cV, 1))
                lg_ms = cV.v
                b = ntr4[0] % 2
                T("transpose", ptr4[b][:], lgpad[:],
                  ident_b[:], wait=[(cV, p4_free[b]), (cV, lg_ms)],
                  inc=(cT, 1))
                lgpad_guard = cT.v
                V("tensor_copy", lgt[:], ptr4[b][:, 0:8],
                  wait=[(cT, cT.v)], inc=(cV, 1))
                p4_free[b] = cV.v
                ntr4[0] += 1
                V("reduce_max", smx[:, 0:1], lgt[:], axis=AX.X, negate=True,
                  wait=[(cV, cV.v)], inc=(cV, 1))
                S("activation", smx[:, 8:16], lgt[:], AF.Exp,
                  bias=smx[:, 0:1], wait=[(cV, cV.v)], inc=(cS, 1))
                V("reduce_sum", smx[:, 1:2], smx[:, 8:16], axis=AX.X,
                  wait=[(cS, cS.v)], inc=(cV, 1))
                V("reciprocal", smx[:, 2:3], smx[:, 1:2],
                  wait=[(cV, cV.v)], inc=(cV, 1))
                V("tensor_scalar", smx[:, 16:24], smx[:, 8:16],
                  smx[:, 2:3], None, ALU.mult, wait=[(cV, cV.v)],
                  inc=(cV, 1))
                V("reduce_max", smx[:, 3:4], smx[:, 16:24], axis=AX.X,
                  wait=[(cV, cV.v)], inc=(cV, 1))
                V("tensor_scalar", smx[:, 24:32], smx[:, 16:24],
                  smx[:, 3:4], None, ALU.is_equal, wait=[(cV, cV.v)],
                  inc=(cV, 1))
                V("scalar_tensor_tensor", smx[:, 32:40], smx[:, 24:32],
                  -2.0, smx[:, 16:24], ALU.mult, ALU.add,
                  wait=[(cV, cV.v)], inc=(cV, 1))
                V("reduce_max", smx[:, 4:5], smx[:, 32:40], axis=AX.X,
                  wait=[(cV, cV.v)], inc=(cV, 1))
                V("tensor_scalar", smx[:, 40:48], smx[:, 16:24],
                  smx[:, 4:5], None, ALU.is_ge, wait=[(cV, cV.v)],
                  inc=(cV, 1))
                V("tensor_tensor", cpad[:, 0:8], smx[:, 16:24],
                  smx[:, 40:48], ALU.mult,
                  wait=[(cV, cV.v), (cT, cpad_tg)], inc=(cV, 1))
                b = ntr4[0] % 2
                T("transpose", ptr4[b][:], cpad[:], ident_b[:],
                  wait=[(cV, p4_free[b]), (cV, cV.v)], inc=(cT, 1))
                cpad_tg = cT.v
                V("tensor_copy",
                  combr[:, c * CH + tt * 128: c * CH + (tt + 1) * 128],
                  ptr4[b][0:8, :], wait=[(cT, cT.v)], inc=(cV, 1))
                p4_free[b] = cV.v
                ntr4[0] += 1
            pR_free = cV.v
        flush()
        psR.close()
        psM = contextlib.ExitStack()
        pH1s = [psM.enter_context(nc.psum_tensor("pH1a", [128, 512], F32)),
                psM.enter_context(nc.psum_tensor("pH1b", [128, 512], F32))]
        pH2s = [psM.enter_context(nc.psum_tensor("pH2a", [128, 512], F32)),
                psM.enter_context(nc.psum_tensor("pH2b", [128, 512], F32))]
        pEs = [psM.enter_context(nc.psum_tensor("pEa", [128, 512], F32)),
               psM.enter_context(nc.psum_tensor("pEb", [128, 512], F32))]
        ms_comb = cV.v
        ms_gbuf = cV.v
        if DEBUG:
            Y("dma_start", out=dbg["comb"], in_=combr[:],
              wait=[(cV, ms_comb)], inc=(SS["dbg"], 16))

        wf_free = [0, 0]
        s1_guard = [0, 0]
        hst_store = [0, 0]
        pH1_sfree = [0, 0]
        pH2_free = [0, 0]
        pE_free = [0, 0]
        gt_guard = 0
        ms_hsh = {}
        hst = hbk.ap()[:, 0:2 * L]
        wfss = [SS["wf0"], SS["wf1"]]
        hsss = [SS["hs0"], SS["hs1"]]
        for m in range(KS):
            b = m % 2
            G("dma_start", out=wfa[:, b * KD * 128:(b + 1) * KD * 128],
              in_=w_s1[m], wait=[(cT, wf_free[b])], inc=(wfss[b], 16))
            G("dma_start", out=wfb[:, b * KD * 128:(b + 1) * KD * 128],
              in_=w_s2[m], inc=(wfss[b], 16))
            ms_wf = wfss[b].v
            hp = m % 2
            for c in range(NCH):
                cp = c % 2
                for k in range(KD):
                    w0 = ([(wfss[b], ms_wf), (SS["xn2l"], ms_xn2r),
                           (cS, pH1_sfree[cp]),
                           (cV, pH2_free[cp])] if k == 0 else ())
                    T("matmul", pH1s[cp][:],
                      wfa[:, b * KD * 128 + k * 128:
                          b * KD * 128 + (k + 1) * 128],
                      xn2[:, k * L + c * CH: k * L + (c + 1) * CH],
                      start=(k == 0), stop=(k == KD - 1), wait=w0,
                      inc=((cT, 1) if k == KD - 1 else None))
                    T("matmul", pH2s[cp][:],
                      wfb[:, b * KD * 128 + k * 128:
                          b * KD * 128 + (k + 1) * 128],
                      xn2[:, k * L + c * CH: k * L + (c + 1) * CH],
                      start=(k == 0), stop=(k == KD - 1),
                      inc=((cT, 1) if k == KD - 1 else None))
                ms_h12 = cT.v
                spp = c % 2
                S("activation", s1b[:, spp * CH:(spp + 1) * CH], pH1s[cp][:],
                  AF.Silu, wait=[(cT, ms_h12), (cV, s1_guard[spp])],
                  inc=(cS, 1))
                pH1_sfree[cp] = cS.v
                V("tensor_tensor", gt[:], s1b[:, spp * CH:(spp + 1) * CH],
                  pH2s[cp][:], ALU.mult, wait=[(cS, cS.v), (cV, gt_guard)],
                  inc=(cV, 1))
                s1_guard[spp] = cV.v
                pH2_free[cp] = cV.v
                V("tensor_tensor",
                  hst[:, hp * L + c * CH: hp * L + (c + 1) * CH],
                  gt[:], gbuf[:, c * CH:(c + 1) * CH], ALU.mult,
                  wait=[(cV, ms_gbuf), (hsss[hp], hst_store[hp]),
                        (cV, cV.v)], inc=(cV, 1))
                gt_guard = cV.v
            wf_free[b] = cT.v
            Y("dma_start", out=hsh_d[m], in_=hst[:, hp * L:(hp + 1) * L],
              wait=[(cV, vfence())], inc=(hsss[hp], 16))
            hst_store[hp] = hsss[hp].v

        flush()
        psG.close()
        cbe = en(nc.sbuf_tensor("cbe", [128, L], BF16))
        wpc_free = [0, 0, 0]
        wpss = [SS["wp0"], SS["wp1"]]
        oass = [SS["oa0"], SS["oa1"], SS["oa2"], SS["oa3"]]
        oacst_guard = [(oass[0], 0)] * 2
        oacst_rr = [0]
        hbk_tguard = 0
        for kg in range(4):
            Y("dma_start",
              out=hbk.ap().rearrange("p (a l) -> p a l", l=L),
              in_=hsh_d[kg * KF:(kg + 1) * KF].transpose([1, 0, 2]),
              wait=[(SS["hs0"], SS["hs0"].v), (SS["hs1"], SS["hs1"].v),
                    (cT, hbk_tguard)],
              inc=(SS["hb"], 16))
            ms_hbk = SS["hb"].v
            for m in range(KD):
                b = m % 3
                G("dma_start", out=wpc[:, b * KF * 128:(b + 1) * KF * 128],
                  in_=w_sp[m][:, kg * KF * 128:(kg + 1) * KF * 128],
                  wait=[(cT, wpc_free[b])], inc=(wpss[b % 2], 16))
                ms_wpc = wpss[b % 2].v
                for c in range(NCH):
                    cp = c % 2
                    for k in range(KF):
                        w0 = ([(wpss[b % 2], ms_wpc), (SS["hb"], ms_hbk),
                               (cV, pE_free[cp])] if k == 0 else ())
                        T("matmul", pEs[cp][:],
                          wpc[:, b * KF * 128 + k * 128:
                              b * KF * 128 + (k + 1) * 128],
                          hbk[:, k * L + c * CH: k * L + (c + 1) * CH],
                          start=(k == 0), stop=(k == KF - 1), wait=w0,
                          inc=((cT, 1) if k == KF - 1 else None))
                    ms_pe = cT.v
                    op = m % 2
                    gsem, gval = oacst_guard[op]
                    V("tensor_copy", oacst[:, op * L + c * CH:
                                           op * L + (c + 1) * CH],
                      pEs[cp][:], wait=[(cT, ms_pe), (gsem, gval)],
                      inc=(cV, 1))
                    pE_free[cp] = cV.v
                grp = oass[m % 4]
                G("dma_start", out=oac_d[m],
                  in_=oacst[:, op * L:(op + 1) * L],
                  wait=[(cV, vfence()), (grp, grp.v)],
                  inc=(grp, 16),
                  **({} if kg == 0 else {"accum_op": ALU.add}))
                oacst_guard[op] = (grp, grp.v)
                wpc_free[b] = cT.v
            hbk_tguard = cT.v

        he_tguard = cT.v
        for e in range(E):
            for c in range(NCH):
                T("matmul", pbc2[:], sel8_b[:, e * 128:(e + 1) * 128],
                  combr[:, c * CH:(c + 1) * CH], start=True, stop=True,
                  wait=[(cV, max(ms_comb, pbc2_free)), (SS["wg4"], ms_wg)],
                  inc=(cT, 1))
                V("tensor_copy", cbe[:, c * CH:(c + 1) * CH], pbc2[:],
                  wait=[(cT, cT.v)], inc=(cV, 1))
                pbc2_free = cV.v
            ms_cbe = cV.v
            for m in range(KF):
                b = m % 2
                G("dma_start", out=wfa[:, b * KD * 128:(b + 1) * KD * 128],
                  in_=w_f1[e, m], wait=[(cT, wf_free[b])],
                  inc=(wfss[b], 16))
                G("dma_start", out=wfb[:, b * KD * 128:(b + 1) * KD * 128],
                  in_=w_f2[e, m], inc=(wfss[b], 16))
                ms_wf = wfss[b].v
                for c in range(NCH):
                    cp = c % 2
                    for k in range(KD):
                        w0 = ([(wfss[b], ms_wf), (SS["xn2l"], ms_xn2r),
                               (cS, pH1_sfree[cp]),
                               (cV, pH2_free[cp])] if k == 0 else ())
                        T("matmul", pH1s[cp][:],
                          wfa[:, b * KD * 128 + k * 128:
                              b * KD * 128 + (k + 1) * 128],
                          xn2[:, k * L + c * CH: k * L + (c + 1) * CH],
                          start=(k == 0), stop=(k == KD - 1), wait=w0,
                          inc=((cT, 1) if k == KD - 1 else None))
                        T("matmul", pH2s[cp][:],
                          wfb[:, b * KD * 128 + k * 128:
                              b * KD * 128 + (k + 1) * 128],
                          xn2[:, k * L + c * CH: k * L + (c + 1) * CH],
                          start=(k == 0), stop=(k == KD - 1),
                          inc=((cT, 1) if k == KD - 1 else None))
                    ms_h12 = cT.v
                    spp = c % 2
                    S("activation", s1b[:, spp * CH:(spp + 1) * CH], pH1s[cp][:],
                      AF.Silu, wait=[(cT, ms_h12), (cV, s1_guard[spp])],
                      inc=(cS, 1))
                    pH1_sfree[cp] = cS.v
                    V("tensor_tensor", gt[:],
                      s1b[:, spp * CH:(spp + 1) * CH], pH2s[cp][:], ALU.mult,
                      wait=[(cS, cS.v), (cV, gt_guard)], inc=(cV, 1))
                    s1_guard[spp] = cV.v
                    pH2_free[cp] = cV.v
                    V("tensor_tensor",
                      hbk[:, m * L + c * CH: m * L + (c + 1) * CH],
                      gt[:], cbe[:, c * CH:(c + 1) * CH], ALU.mult,
                      wait=[(cV, ms_cbe), (cT, he_tguard), (cV, cV.v)],
                      inc=(cV, 1))
                    gt_guard = cV.v
                wf_free[b] = cT.v
            ms_he = cV.v
            for m2 in range(KD):
                b = m2 % 3
                G("dma_start", out=wpc[:, b * KF * 128:(b + 1) * KF * 128],
                  in_=w_ep[e, m2], wait=[(cT, wpc_free[b])],
                  inc=(wpss[b % 2], 16))
                ms_wpc = wpss[b % 2].v
                for c in range(NCH):
                    cp = c % 2
                    for k in range(KF):
                        w0 = ([(wpss[b % 2], ms_wpc), (cV, ms_he),
                               (cV, pE_free[cp])] if k == 0 else ())
                        T("matmul", pEs[cp][:],
                          wpc[:, b * KF * 128 + k * 128:
                              b * KF * 128 + (k + 1) * 128],
                          hbk[:, k * L + c * CH: k * L + (c + 1) * CH],
                          start=(k == 0), stop=(k == KF - 1), wait=w0,
                          inc=((cT, 1) if k == KF - 1 else None))
                    ms_pe = cT.v
                    op = m2 % 2
                    gsem, gval = oacst_guard[op]
                    V("tensor_copy", oacst[:, op * L + c * CH:
                                           op * L + (c + 1) * CH],
                      pEs[cp][:], wait=[(cT, ms_pe), (gsem, gval)],
                      inc=(cV, 1))
                    pE_free[cp] = cV.v
                grp = oass[m2 % 4]
                G("dma_start", out=oac_d[m2],
                  in_=oacst[:, op * L:(op + 1) * L],
                  wait=[(cV, vfence()), (grp, grp.v)],
                  inc=(grp, 16), accum_op=ALU.add)
                oacst_guard[op] = (grp, grp.v)
                wpc_free[b] = cT.v
            he_tguard = cT.v

        fo_sguard = 0
        x1t2_vg = 0
        flush()
        psM.close()
        psF = contextlib.ExitStack()
        ptrE = psF.enter_context(nc.psum_tensor("ptrE", [128, 128], BF16))
        ptrF = psF.enter_context(nc.psum_tensor("ptrF", [128, 128], BF16))
        ptr4 = [ptrE, ptrF]
        p4_free = [0, 0]
        oa_full = [(s, s.v) for s in oass]
        oacr_tg = [0, 0]
        oacr_ld = [0, 0]
        for ti in range(L // 128 + 1):
            if ti < L // 128:
                ob = ti % 2
                G("dma_start",
                  out=oacr.ap()[:, ob * KD * 128:(ob + 1) * KD * 128]
                  .rearrange("p (c t) -> p c t", t=128),
                  in_=oac_d[:, :, ti * 128:(ti + 1) * 128]
                  .transpose([1, 0, 2]),
                  wait=oa_full + [(cT, oacr_tg[ob])], inc=(SS["oar"], 16))
                oacr_ld[ob] = SS["oar"].v
            if ti == 0:
                continue
            tj = ti - 1
            ob = tj % 2
            Y("dma_start", out=x1t2[:],
              in_=x1_d[tj * 128:(tj + 1) * 128, :],
              wait=[(SS["x1s"], SS["x1s"].v), (cV, x1t2_vg)],
              inc=(SS["x12"], 16))
            ms_x1l = SS["x12"].v
            for ct in range(KD):
                b = ntr4[0] % 2
                T("transpose", ptr4[b][:],
                  oacr[:, ob * KD * 128 + ct * 128:
                       ob * KD * 128 + (ct + 1) * 128],
                  ident_b[:],
                  wait=[(cV, p4_free[b]), (SS["oar"], oacr_ld[ob])],
                  inc=(cT, 1))
                V("tensor_copy", fotok[:, ct * 128:(ct + 1) * 128],
                  ptr4[b][:], wait=[(cT, cT.v), (cS, fo_sguard)],
                  inc=(cV, 1))
                p4_free[b] = cV.v
                ntr4[0] += 1
            oacr_tg[ob] = cT.v
            V(None, wait=[(cV, cV.v)])
            S(None, wait=[(cV, cV.v), (cS, fo_sguard)])
            ln_tok(fotok[:], lnsq[:, 0:D], D)
            fo_sguard = cS.v
            V("tensor_tensor", otok[:], lnsq[:, 0:D], x1t2[:], ALU.add,
              wait=[(SS["x12"], ms_x1l), (SS["outs"], SS["outs"].v),
                    (cV, cV.v)], inc=(cV, 1))
            x1t2_vg = cV.v
            Y("dma_start", out=out[tj * 128:(tj + 1) * 128, :],
              in_=otok[:], wait=[(cV, vfence())], inc=(SS["outs"], 16))
        flush()
        psF.close()
        p4.close()
        Y(None, wait=[(s, s.v) for s in SS.values()] + [(sq, sq.v),
                                                        (gq, gq.v)])
        flush()

    return nc


def prep_shared(inputs):
    f = {}
    f["w_in"] = tile_lhsT(inputs["L0_attn_in_proj_w"], KD, 2 * KI)
    xp = np.asarray(inputs["L0_attn_x_proj_w"], np.float32)
    f["w_xdt"] = tile_lhsT(xp[0:128], KI, 1)[0]
    f["w_xB"] = tile_lhsT(xp[128:144], KI, 1, msz=16)[0]
    f["w_xC"] = tile_lhsT(xp[144:160], KI, 1, msz=16)[0]
    f["w_dtp"] = tile_lhsT(inputs["L0_attn_dt_proj_w"], 1, KI)
    wo = tile_lhsT(inputs["L0_attn_out_proj_w"], KI, KD)
    wo = wo.reshape(2, 8, 128, KI, 128).transpose(3, 0, 2, 1, 4)
    f["w_out_k"] = _t(wo.reshape(KI, 2, 128, 8 * 128))
    f["w_gate"] = tile_lhsT(inputs["L0_ffn_gate_w"], KD, 1, msz=8)[0]
    f["w_sgate"] = tile_lhsT(inputs["L0_ffn_shared_gate_w"], KD, 1, msz=1)[0]
    f["w_f1"] = np.stack([tile_lhsT(inputs["L0_ffn_expert_fc1"][e], KD, KF)
                          for e in range(E)])
    f["w_f2"] = np.stack([tile_lhsT(inputs["L0_ffn_expert_fc2"][e], KD, KF)
                          for e in range(E)])
    f["w_ep"] = np.stack([tile_lhsT(inputs["L0_ffn_expert_proj"][e], KF, KD)
                          for e in range(E)])
    f["w_s1"] = tile_lhsT(inputs["L0_ffn_shared_fc1"], KD, KS)
    f["w_s2"] = tile_lhsT(inputs["L0_ffn_shared_fc2"], KD, KS)
    f["w_sp"] = tile_lhsT(inputs["L0_ffn_shared_proj"], KS, KD)
    cw = np.asarray(inputs["L0_attn_conv_w"], np.float32)
    f["c_w"] = _t(cw.reshape(KI, 128, 4).transpose(1, 0, 2)
                  .reshape(128, KI * 4))
    f["c_b"] = col128(inputs["L0_attn_conv_b"], KI)
    f["dt_b"] = col128(inputs["L0_attn_dt_proj_b"], KI)
    al = np.asarray(inputs["L0_attn_A_log"], np.float32)
    f["a_log"] = _t(al.reshape(KI, 128, DS).transpose(1, 0, 2)
                    .reshape(128, KI * DS))
    f["d_vec"] = col128(inputs["L0_attn_D"], KI)
    f["ident"] = np.eye(128, dtype=ml_dtypes.bfloat16)
    s16 = np.zeros((16, 16 * 128), dtype=ml_dtypes.bfloat16)
    for s in range(16):
        s16[s, s * 128:(s + 1) * 128] = 1
    f["sel16"] = s16
    s8 = np.zeros((8, 8 * 128), dtype=ml_dtypes.bfloat16)
    for s in range(8):
        s8[s, s * 128:(s + 1) * 128] = 1
    f["sel8"] = s8
    f["ones1"] = np.ones((1, 128), dtype=ml_dtypes.bfloat16)
    return f


def kernel(**inputs):
    x = np.asarray(inputs["L0_attn_hidden_states"], np.float32)
    shared = prep_shared(inputs)
    nc = build_nc()
    in_maps = [dict(shared, x=_t(x[b])) for b in range(8)]
    res = run_bass_kernel_spmd(nc, in_maps, core_ids=list(range(8)))
    outs = np.stack([np.asarray(res.results[b]["out"], np.float32)
                     for b in range(8)], axis=0)
    return outs


# version 22
# speedup vs baseline: 1.2912x; 1.2912x over previous
import contextlib
import numpy as np
import ml_dtypes
import concourse.bass as bass
import concourse.mybir as mybir
from concourse.bass_utils import run_bass_kernel_spmd

F32 = mybir.dt.float32
BF16 = mybir.dt.bfloat16
AF = mybir.ActivationFunctionType
ALU = mybir.AluOpType
AX = mybir.AxisListType

L = 2048
D = 2048
DI = 4096
DS = 16
E = 8
FM = 1408
FS = 5632
CH = 512
NCH = L // CH
NT = CH // 128
KD = D // 128
KI = DI // 128
KF = FM // 128
KS = FS // 128
EPS = 1e-5

DEBUG = False


def _t(x):
    return np.ascontiguousarray(x)


def tile_lhsT(W, nk, nm, msz=128):
    WT = np.ascontiguousarray(np.asarray(W, np.float32).T)
    nin, nout = WT.shape
    assert nin == nk * 128 and nout == nm * msz, (W.shape, nk, nm, msz)
    return _t(WT.reshape(nk, 128, nm, msz).transpose(2, 1, 0, 3)
              .reshape(nm, 128, nk * msz))


def col128(v, nt):
    return _t(np.asarray(v, np.float32).reshape(nt, 128).T)


class Ctr:
    def __init__(self, h):
        self.h = h
        self.v = 0


def build_nc():
    nc = bass.Bass("TRN2", target_bir_lowering=False)

    def din(name, shape, dtype=F32):
        return nc.dram_tensor(name, shape, dtype, kind="ExternalInput").ap()

    def dout(name, shape, dtype=F32):
        return nc.dram_tensor(name, shape, dtype, kind="ExternalOutput").ap()

    def dint(name, shape, dtype=F32):
        kind = "ExternalOutput" if DEBUG else "Internal"
        return nc.dram_tensor(name, shape, dtype, kind=kind).ap()

    x_in = din("x", [L, D])
    w_in = din("w_in", [2 * KI, 128, KD * 128])
    w_xdt = din("w_xdt", [128, KI * 128])
    w_xB = din("w_xB", [128, KI * 16])
    w_xC = din("w_xC", [128, KI * 16])
    w_dtp = din("w_dtp", [KI, 128, 128])
    w_out_k = din("w_out_k", [KI, 2, 128, 8 * 128])
    w_gate = din("w_gate", [128, KD * 8])
    w_sgate = din("w_sgate", [128, KD])
    w_f1 = din("w_f1", [E, KF, 128, KD * 128])
    w_f2 = din("w_f2", [E, KF, 128, KD * 128])
    w_ep = din("w_ep", [E, KD, 128, KF * 128])
    w_s1 = din("w_s1", [KS, 128, KD * 128])
    w_s2 = din("w_s2", [KS, 128, KD * 128])
    w_sp = din("w_sp", [KD, 128, KS * 128])
    c_w = din("c_w", [128, KI * 4])
    c_b = din("c_b", [128, KI])
    dt_b = din("dt_b", [128, KI])
    a_log = din("a_log", [128, KI * DS])
    d_vec = din("d_vec", [128, KI])
    ident = din("ident", [128, 128], BF16)
    sel16 = din("sel16", [16, 16 * 128], BF16)
    sel8 = din("sel8", [8, 8 * 128], BF16)
    ones1 = din("ones1", [1, 128], BF16)

    out = dout("out", [L, D])
    xact_d = dint("xact_d", [KI, 128, L], BF16)
    z_d = dint("z_d", [KI, 128, L], BF16)
    yop_d = dint("yop_d", [KI, 128, L], BF16)
    x1_d = dint("x1_d", [L, D], F32)
    xn2_d = dint("xn2_d", [L // 128, KD, 128, 128], BF16)
    hsh_d = dint("hsh_d", [KS, 128, L], BF16)
    oac_d = dint("oac_d", [KD, 128, L], F32)
    dbg = {}
    if DEBUG:
        dbg["xn1"] = dout("dbg_xn1", [KD, 128, L], BF16)
        dbg["dt"] = dout("dbg_dt", [KI, 128, L], F32)
        dbg["ssm"] = dout("dbg_ssm", [KI, 128, L], F32)
        dbg["comb"] = dout("dbg_comb", [8, L], BF16)
        dbg["hbig1"] = dout("dbg_hbig1", [128, DS * CH], BF16)
        dbg["hcb1"] = dout("dbg_hcb1", [128, DS * CH], BF16)

    ctx = contextlib.ExitStack()
    with ctx:
        sb = ctx.enter_context
        gq = Ctr(sb(nc.semaphore("gq")))
        sq = Ctr(sb(nc.semaphore("sq")))
        cT = Ctr(sb(nc.semaphore("cT")))
        cV = Ctr(sb(nc.semaphore("cV")))
        cS = Ctr(sb(nc.semaphore("cS")))
        cP = Ctr(sb(nc.semaphore("cP")))
        SS = {}
        for _sn in ["xt", "wc", "xa0", "xa1", "z0", "z1", "dbg",
                    "xp0", "xp1", "xp2", "xc0", "xc1", "zb0", "zb1",
                    "yo0", "yo1", "yk0", "yk1", "yk2", "wk0", "wk1", "wk2",
                    "x1s", "x2s", "xt2", "wf0", "wf1", "wp0", "wp1",
                    "hs0", "hs1", "hb", "oa0", "oa1", "oa2", "oa3",
                    "oar", "outs", "x12", "wsm", "wg4", "glb", "xn2l"]:
            SS[_sn] = Ctr(sb(nc.semaphore("s_" + _sn)))
        block = ctx.enter_context(nc.Block())

        pend = {}

        def flush():
            for attr, fns in pend.items():
                if fns:
                    def body(e, fns=tuple(fns)):
                        for f in fns:
                            f(e)
                    getattr(block, attr)(body)
            pend.clear()

        def mk_emit(attr):
            def emit(op, *args, wait=(), inc=None, **kw):
                wl = []
                seen = set()
                for (c0, val) in wait:
                    if val > 0 and (id(c0), val) not in seen:
                        seen.add((id(c0), val))
                        wl.append((c0.h, val))
                if inc is not None:
                    c0, n = inc
                    ih, inn = c0.h, n
                    c0.v += n
                else:
                    ih = None
                    inn = 0

                def one(e, op=op, args=args, kw=kw, wl=tuple(wl), ih=ih,
                        inn=inn):
                    for (h, val) in wl:
                        e.wait_ge(h, val)
                    if op is not None:
                        inst = getattr(e, op)(*args, **kw)
                        if ih is not None:
                            inst.then_inc(ih, inn)
                pend.setdefault(attr, []).append(one)
            return emit

        T = mk_emit("tensor")
        V = mk_emit("vector")
        S = mk_emit("scalar")
        G = mk_emit("gpsimd")
        Y = mk_emit("sync")

        ident_b = sb(nc.sbuf_tensor("ident_b", [128, 128], BF16))
        ones1_b = sb(nc.sbuf_tensor("ones1_b", [1, 128], BF16))
        cw_sb = sb(nc.sbuf_tensor("cw_sb", [128, KI * 4], F32))
        cb_sb = sb(nc.sbuf_tensor("cb_sb", [128, KI], F32))
        dtb2_sb = sb(nc.sbuf_tensor("dtb2_sb", [128, KI], F32))
        A_sb = sb(nc.sbuf_tensor("A_sb", [128, KI * DS], F32))
        Dv_sb = sb(nc.sbuf_tensor("Dv_sb", [128, KI], F32))
        state_sb = sb(nc.sbuf_tensor("state_sb", [128, KI * DS], F32))
        stat_sb = sb(nc.sbuf_tensor("stat_sb", [128, 16], F32))
        lnsq = sb(nc.sbuf_tensor("lnsq", [128, D], F32))
        cst = sb(nc.sbuf_tensor("cst", [128, 2], F32))

        for dst, src in [(ident_b, ident), (ones1_b, ones1), (cw_sb, c_w),
                         (cb_sb, c_b), (dtb2_sb, dt_b), (A_sb, a_log),
                         (Dv_sb, d_vec)]:
            G("dma_start", out=dst[:], in_=src, inc=(SS["glb"], 16))
        glob_ld = SS["glb"].v
        S("activation", A_sb[:], A_sb[:], AF.Exp,
          wait=[(SS["glb"], glob_ld)], inc=(cS, 1))
        V("tensor_scalar", A_sb[:], A_sb[:], -1.0, None, ALU.mult,
          wait=[(cS, cS.v)], inc=(cV, 1))
        V("tensor_scalar", dtb2_sb[:], dtb2_sb[:], 2.0, None, ALU.mult,
          wait=[(SS["glb"], glob_ld)], inc=(cV, 1))
        V("memset", state_sb[:], 0.0, inc=(cV, 1))
        V("memset", cst[:, 0:1], EPS, inc=(cV, 1))
        V("memset", cst[:, 1:2], 1.0, inc=(cV, 1))
        ms_glob_v = cV.v
        ms_glob_s = cS.v

        fence_col = stat_sb

        def vfence():
            V("memset", fence_col[:, 15:16], 0.0, inc=(cV, 1))
            return cV.v

        def ln_tok(src_ap, dst_ap, dim):
            S("activation", lnsq[:, 0:dim], src_ap, AF.Square, inc=(cS, 1))
            sqv = cS.v
            V("reduce_sum", stat_sb[:, 0:1], src_ap, axis=AX.X, inc=(cV, 1))
            V("reduce_sum", stat_sb[:, 1:2], lnsq[:, 0:dim], axis=AX.X,
              wait=[(cS, sqv), (cV, cV.v)], inc=(cV, 1))
            V("tensor_scalar", stat_sb[:, 2:3], stat_sb[:, 0:1], 1.0 / dim,
              None, ALU.mult, wait=[(cV, cV.v)], inc=(cV, 1))
            V("tensor_tensor", stat_sb[:, 3:4], stat_sb[:, 2:3],
              stat_sb[:, 2:3], ALU.mult, wait=[(cV, cV.v)], inc=(cV, 1))
            V("scalar_tensor_tensor", stat_sb[:, 4:5], stat_sb[:, 1:2],
              1.0 / dim, stat_sb[:, 3:4], ALU.mult, ALU.subtract,
              wait=[(cV, cV.v)], inc=(cV, 1))
            S("activation", stat_sb[:, 5:6], stat_sb[:, 4:5], AF.Ln,
              bias=cst[:, 0:1], wait=[(cV, cV.v)], inc=(cS, 1))
            S("activation", stat_sb[:, 6:7], stat_sb[:, 5:6], AF.Exp,
              scale=-0.5, wait=[(cS, cS.v)], inc=(cS, 1))
            rstdv = cS.v
            V("tensor_scalar", stat_sb[:, 7:8], stat_sb[:, 2:3], -1.0, None,
              ALU.mult, inc=(cV, 1))
            V("tensor_scalar", dst_ap, src_ap, stat_sb[:, 7:8],
              stat_sb[:, 6:7], ALU.add, ALU.mult,
              wait=[(cS, rstdv), (cV, cV.v)], inc=(cV, 1))

        p01 = contextlib.ExitStack()
        en = p01.enter_context
        xtok = en(nc.sbuf_tensor("xtok", [128, 2 * D], F32))
        xn1tok = en(nc.sbuf_tensor("xn1tok", [128, D], BF16))
        xn1 = en(nc.sbuf_tensor("xn1", [128, KD * L], BF16))
        wcol = en(nc.sbuf_tensor("wcol", [128, 2 * KD * 128], BF16))
        xw = en(nc.sbuf_tensor("xw", [128, 2 * 520], F32))
        cacc = en(nc.sbuf_tensor("cacc", [128, 2 * CH], F32))
        xsil = en(nc.sbuf_tensor("xsil", [128, 2 * CH], BF16))
        zev = en(nc.sbuf_tensor("zev", [128, 2 * CH], BF16))
        ptr0 = en(nc.psum_tensor("ptr0", [128, 128], BF16))
        ptr1 = en(nc.psum_tensor("ptr1", [128, 128], BF16))
        pmm0 = en(nc.psum_tensor("pmm0", [128, 512], F32))
        pmm1 = en(nc.psum_tensor("pmm1", [128, 512], F32))

        ptr = [ptr0, ptr1]
        ptr_free = [0, 0]
        ntr = [0]

        def transpose_p1(dst_ap, src_ap, src_ms=0):
            b = ntr[0] % 2
            T("transpose", ptr[b][:], src_ap, ident_b[:],
              wait=[(cV, ptr_free[b]), (cV, src_ms), (SS["glb"], glob_ld)],
              inc=(cT, 1))
            V("tensor_copy", dst_ap, ptr[b][:], wait=[(cT, cT.v)],
              inc=(cV, 1))
            ptr_free[b] = cV.v
            ntr[0] += 1

        xn1tok_tg = 0
        xtok_free = [0, 0]
        for i in range(L // 128):
            b = i % 2
            Y("dma_start", out=xtok[:, b * D:(b + 1) * D],
              in_=x_in[i * 128:(i + 1) * 128, :],
              wait=[(cV, xtok_free[b])], inc=(SS["xt"], 16))
            V(None, wait=[(SS["xt"], SS["xt"].v), (cT, xn1tok_tg)])
            S(None, wait=[(SS["xt"], SS["xt"].v)])
            ln_tok(xtok[:, b * D:(b + 1) * D], xn1tok[:], D)
            xtok_free[b] = cV.v
            xn1tok_ms = cV.v
            for ct in range(KD):
                transpose_p1(
                    xn1[:, ct * L + i * 128: ct * L + (i + 1) * 128],
                    xn1tok[:, ct * 128:(ct + 1) * 128], src_ms=xn1tok_ms)
            xn1tok_tg = cT.v
        ms_xn1 = cV.v
        if DEBUG:
            Y("dma_start", out=dbg["xn1"].transpose([1, 0, 2]),
              in_=xn1.ap().rearrange("p (c l) -> p c l", l=L),
              wait=[(cV, ms_xn1)], inc=(SS["dbg"], 16))

        pmm = [pmm0, pmm1]
        pmm_free = [0, 0]
        wcol_free = [0, 0]
        ms_xact = {}
        ms_z = {}
        conv_guard = 0
        xsil_store = [0, 0]
        zev_store = [0, 0]
        for m in range(2 * KI):
            b = m % 2
            G("dma_start", out=wcol[:, b * KD * 128:(b + 1) * KD * 128],
              in_=w_in[m], wait=[(cT, wcol_free[b])], inc=(SS["wc"], 16))
            ms_wm = SS["wc"].v
            is_x = m < KI
            d = m if is_x else m - KI
            if is_x:
                wb = d % 2
                V("memset", xw[:, wb * 520: wb * 520 + 4], 0.0, inc=(cV, 1))
            for c in range(NCH):
                pb = (m * NCH + c) % 2
                for k in range(KD):
                    w0 = ([(SS["wc"], ms_wm), (cV, ms_xn1),
                           (cV, pmm_free[pb])] if k == 0 else ())
                    T("matmul", pmm[pb][:],
                      wcol[:, b * KD * 128 + k * 128:
                           b * KD * 128 + (k + 1) * 128],
                      xn1[:, k * L + c * CH: k * L + (c + 1) * CH],
                      start=(k == 0), stop=(k == KD - 1), wait=w0,
                      inc=((cT, 1) if k == KD - 1 else None))
                mm_done = cT.v
                if c == NCH - 1:
                    wcol_free[b] = mm_done
                if is_x:
                    V("tensor_copy", xw[:, wb * 520 + 4: wb * 520 + 4 + CH],
                      pmm[pb][:], wait=[(cT, mm_done)], inc=(cV, 1))
                    pmm_free[pb] = cV.v
                    ca0 = cacc[:, 0:CH]
                    ca1 = cacc[:, CH:2 * CH]
                    V("tensor_scalar", ca0,
                      xw[:, wb * 520 + 1: wb * 520 + 1 + CH],
                      cw_sb[:, d * 4: d * 4 + 1], None, ALU.mult,
                      wait=[(cV, cV.v)], inc=(cV, 1))
                    V("scalar_tensor_tensor", ca1,
                      xw[:, wb * 520 + 2: wb * 520 + 2 + CH],
                      cw_sb[:, d * 4 + 1: d * 4 + 2], ca0,
                      ALU.mult, ALU.add, wait=[(cS, conv_guard)],
                      inc=(cV, 1))
                    V("scalar_tensor_tensor", ca0,
                      xw[:, wb * 520 + 3: wb * 520 + 3 + CH],
                      cw_sb[:, d * 4 + 2: d * 4 + 3], ca1,
                      ALU.mult, ALU.add, inc=(cV, 1))
                    V("scalar_tensor_tensor", ca1,
                      xw[:, wb * 520 + 4: wb * 520 + 4 + CH],
                      cw_sb[:, d * 4 + 3: d * 4 + 4], ca0,
                      ALU.mult, ALU.add, inc=(cV, 1))
                    conv_done = cV.v
                    V("tensor_copy", xw[:, wb * 520: wb * 520 + 4],
                      xw[:, wb * 520 + CH: wb * 520 + CH + 4], inc=(cV, 1))
                    spp = (m * NCH + c) % 2
                    sxa = SS["xa0"] if spp == 0 else SS["xa1"]
                    S("activation", xsil[:, spp * CH:(spp + 1) * CH], ca1,
                      AF.Silu, bias=cb_sb[:, d:d + 1],
                      wait=[(cV, conv_done), (sxa, xsil_store[spp])],
                      inc=(cS, 1))
                    conv_guard = cS.v
                    Y("dma_start", out=xact_d[d, :, c * CH:(c + 1) * CH],
                      in_=xsil[:, spp * CH:(spp + 1) * CH],
                      wait=[(cS, cS.v)], inc=(sxa, 16))
                    xsil_store[spp] = sxa.v
                else:
                    zp = (m * NCH + c) % 2
                    szv = SS["z0"] if zp == 0 else SS["z1"]
                    V("tensor_copy", zev[:, zp * CH:(zp + 1) * CH],
                      pmm[pb][:],
                      wait=[(cT, mm_done), (szv, zev_store[zp])],
                      inc=(cV, 1))
                    pmm_free[pb] = cV.v
                    Y("dma_start", out=z_d[d, :, c * CH:(c + 1) * CH],
                      in_=zev[:, zp * CH:(zp + 1) * CH],
                      wait=[(cV, vfence())], inc=(szv, 16))
                    zev_store[zp] = szv.v
        flush()
        p01.close()

        p2 = contextlib.ExitStack()
        en = p2.enter_context
        wxdt_sb = en(nc.sbuf_tensor("wxdt_sb", [128, KI * 128], BF16))
        wxB_sb = en(nc.sbuf_tensor("wxB_sb", [128, KI * 16], BF16))
        wxC_sb = en(nc.sbuf_tensor("wxC_sb", [128, KI * 16], BF16))
        wdtp_sb = en(nc.sbuf_tensor("wdtp_sb", [128, KI * 128], BF16))
        sel16_b = en(nc.sbuf_tensor("sel16_b", [16, 16 * 128], BF16))
        xps = en(nc.sbuf_tensor("xps", [128, 3 * CH], BF16))
        xsc = en(nc.sbuf_tensor("xsc", [128, 2 * CH], BF16))
        dtlow = en(nc.sbuf_tensor("dtlow", [128, CH], BF16))
        bcsb = en(nc.sbuf_tensor("bcsb", [16, 2 * CH], BF16))
        Bp = en(nc.sbuf_tensor("Bp", [128, DS * CH], BF16))
        Cp = en(nc.sbuf_tensor("Cp", [128, DS * CH], BF16))
        spu = en(nc.sbuf_tensor("spu", [128, 2 * CH], F32))
        spt = en(nc.sbuf_tensor("spt", [128, 2 * CH], F32))
        dtf = en(nc.sbuf_tensor("dtf", [128, 2 * CH], F32))
        u2 = en(nc.sbuf_tensor("u2", [128, 2 * CH], BF16))
        da = en(nc.sbuf_tensor("da", [128, 2 * CH], F32))
        db = en(nc.sbuf_tensor("db", [128, 3 * CH], BF16))
        hbig = en(nc.sbuf_tensor("hbig", [128, DS * CH], BF16))
        hcb = en(nc.sbuf_tensor("hcb", [128, DS * CH], BF16))
        ysum = en(nc.sbuf_tensor("ysum", [128, CH], F32))
        yt = en(nc.sbuf_tensor("yt", [128, CH], F32))
        zbb = en(nc.sbuf_tensor("zbb", [128, 2 * CH], BF16))
        yob = en(nc.sbuf_tensor("yob", [128, 2 * CH], BF16))
        wko = en(nc.sbuf_tensor("wko", [128, 3 * 8 * 128], BF16))
        yks = en(nc.sbuf_tensor("yks", [128, 3 * CH], BF16))
        moev = en(nc.sbuf_tensor("moev", [128, KD * CH], BF16))
        motok = en(nc.sbuf_tensor("motok", [128, NT * D], BF16))
        xtok2 = en(nc.sbuf_tensor("xtok2", [128, D], F32))
        x1tok = en(nc.sbuf_tensor("x1tok", [128, D], F32))
        xn2tok = en(nc.sbuf_tensor("xn2tok", [128, D], BF16))
        stage16 = en(nc.sbuf_tensor("stage16", [128, KD * 128], BF16))
        tr1 = motok.ap()[:, 0:8 * CH]
        tr2 = motok.ap()[:, 8 * CH:12 * CH]

        G("dma_start", out=wxdt_sb[:], in_=w_xdt, inc=(SS["wsm"], 16))
        G("dma_start", out=wxB_sb[:], in_=w_xB, inc=(SS["wsm"], 16))
        G("dma_start", out=wxC_sb[:], in_=w_xC, inc=(SS["wsm"], 16))
        G("dma_start",
          out=wdtp_sb.ap().rearrange("b (a c) -> b a c", c=128),
          in_=w_dtp.transpose([1, 0, 2]), inc=(SS["wsm"], 16))
        G("dma_start", out=sel16_b[:], in_=sel16, inc=(SS["wsm"], 16))
        ms_wsm = SS["wsm"].v

        ms_yop = {}
        ms_x1row = {}
        dt_guard = [0, 0]
        sp_guard = [0, 0]
        sp_vguard = [0, 0]
        da_guard = [0, 0]
        zb_guard = [0, 0]
        zb_ld = [0, 0]
        yo_store = [0, 0]
        xsc_guard = [0, 0]
        xsc_ld = [0, 0]
        motok_sguard = 0
        hcb_vguard = 0
        tr_vguard = 0
        pool_s_ms = [0] * DS
        x1_sguard = 0
        xt2_vguard = 0
        xn2tok_tg = 0
        st16_sq = 0
        wko_guard = 0
        last_xn2_store = 0

        for c in range(NCH):
            psA = contextlib.ExitStack()
            pdt = psA.enter_context(nc.psum_tensor(f"pdt{c}", [128, 512], F32))
            pB = psA.enter_context(nc.psum_tensor(f"pB{c}", [16, 512], F32))
            pC = psA.enter_context(nc.psum_tensor(f"pC{c}", [16, 512], F32))
            pbc = psA.enter_context(nc.psum_tensor(f"pbc{c}", [128, 512], F32))
            T(None, wait=[(cV, cV.v), (cS, cS.v)])
            xps_ld = [0, 0, 0]
            xps_guard = [0, 0, 0]
            xpss = [SS["xp0"], SS["xp1"], SS["xp2"]]
            for k in range(3):
                Y("dma_start", out=xps[:, k * CH:(k + 1) * CH],
                  in_=xact_d[k, :, c * CH:(c + 1) * CH],
                  wait=[(SS["xa0"], SS["xa0"].v), (SS["xa1"], SS["xa1"].v)],
                  inc=(xpss[k], 16))
                xps_ld[k] = xpss[k].v
            for k in range(KI):
                sl = k % 3
                w0 = [(xpss[sl], xps_ld[sl]), (SS["wsm"], ms_wsm)]
                T("matmul", pdt[:], wxdt_sb[:, k * 128:(k + 1) * 128],
                  xps[:, sl * CH:(sl + 1) * CH], start=(k == 0),
                  stop=(k == KI - 1), wait=w0,
                  inc=((cT, 1) if k == KI - 1 else None))
                T("matmul", pB[:], wxB_sb[:, k * 16:(k + 1) * 16],
                  xps[:, sl * CH:(sl + 1) * CH], start=(k == 0),
                  stop=(k == KI - 1),
                  inc=((cT, 1) if k == KI - 1 else None))
                T("matmul", pC[:], wxC_sb[:, k * 16:(k + 1) * 16],
                  xps[:, sl * CH:(sl + 1) * CH], start=(k == 0),
                  stop=(k == KI - 1), inc=(cT, 1))
                xps_guard[sl] = cT.v
                if k + 3 < KI:
                    Y("dma_start", out=xps[:, sl * CH:(sl + 1) * CH],
                      in_=xact_d[k + 3, :, c * CH:(c + 1) * CH],
                      wait=[(cT, xps_guard[sl])], inc=(xpss[sl], 16))
                    xps_ld[sl] = xpss[sl].v
            ms_xproj = cT.v
            V("tensor_copy", dtlow[:], pdt[:], wait=[(cT, ms_xproj)],
              inc=(cV, 1))
            ms_dtlow = cV.v
            V("tensor_copy", bcsb[:, 0:CH], pB[:], wait=[(cT, ms_xproj)],
              inc=(cV, 1))
            V("tensor_copy", bcsb[:, CH:2 * CH], pC[:],
              wait=[(cT, ms_xproj)], inc=(cV, 1))
            ms_bc = cV.v
            for s in range(DS):
                T("matmul", pbc[:], sel16_b[:, s * 128:(s + 1) * 128],
                  bcsb[:, 0:CH], start=True, stop=True,
                  wait=[(cV, ms_bc), (cV, cV.v)], inc=(cT, 1))
                V("tensor_copy", Bp[:, s * CH:(s + 1) * CH], pbc[:],
                  wait=[(cT, cT.v)], inc=(cV, 1))
                T("matmul", pbc[:], sel16_b[:, s * 128:(s + 1) * 128],
                  bcsb[:, CH:2 * CH], start=True, stop=True,
                  wait=[(cV, cV.v)], inc=(cT, 1))
                V("tensor_copy", Cp[:, s * CH:(s + 1) * CH], pbc[:],
                  wait=[(cT, cT.v)], inc=(cV, 1))
            ms_bcp = cV.v
            flush()
            psA.close()

            psB = contextlib.ExitStack()
            pdd = psB.enter_context(nc.psum_tensor(f"pdd{c}", [128, 512], F32))
            T(None, wait=[(cV, cV.v), (cS, cS.v)])
            pdd_free = 0
            for d in range(KI):
                wb = d % 2
                T("matmul", pdd[:], wdtp_sb[:, d * 128:(d + 1) * 128],
                  dtlow[:], start=True, stop=True,
                  wait=[(cV, pdd_free), (cV, ms_dtlow), (SS["wsm"], ms_wsm)],
                  inc=(cT, 1))
                ms_dd = cT.v
                uo = wb * CH
                V("tensor_scalar", spu[:, uo:uo + CH], pdd[:],
                  dtb2_sb[:, d:d + 1], None, ALU.add,
                  wait=[(cT, ms_dd), (cS, sp_guard[wb]), (cV, ms_glob_v)],
                  inc=(cV, 1))
                ms_u = cV.v
                pdd_free = cV.v
                S("activation", spt[:, uo:uo + CH], spu[:, uo:uo + CH],
                  AF.Abs, wait=[(cV, ms_u), (cV, sp_vguard[wb])],
                  inc=(cS, 1))
                sp_guard[wb] = cS.v
                S("activation", spt[:, uo:uo + CH], spt[:, uo:uo + CH],
                  AF.Exp, scale=-1.0, wait=[(cS, cS.v)], inc=(cS, 1))
                S("activation", spt[:, uo:uo + CH], spt[:, uo:uo + CH],
                  AF.Ln, bias=cst[:, 1:2], wait=[(cS, cS.v)], inc=(cS, 1))
                ms_lnx = cS.v
                V("tensor_scalar", spu[:, uo:uo + CH], spu[:, uo:uo + CH],
                  0.0, None, ALU.max, wait=[(cS, sp_guard[wb])],
                  inc=(cV, 1))
                V("tensor_tensor", dtf[:, uo:uo + CH], spu[:, uo:uo + CH],
                  spt[:, uo:uo + CH], ALU.add,
                  wait=[(cS, max(ms_lnx, dt_guard[wb]))], inc=(cV, 1))
                ms_dt = cV.v
                sp_vguard[wb] = cV.v
                if DEBUG:
                    Y("dma_start", out=dbg["dt"][d, :, c * CH:(c + 1) * CH],
                      in_=dtf[:, uo:uo + CH], wait=[(cV, ms_dt)],
                      inc=(SS["dbg"], 16))
                    V(None, wait=[(SS["dbg"], SS["dbg"].v)])
                xp2 = d % 2
                sxc = SS["xc0"] if xp2 == 0 else SS["xc1"]
                Y("dma_start", out=xsc[:, xp2 * CH:(xp2 + 1) * CH],
                  in_=xact_d[d, :, c * CH:(c + 1) * CH],
                  wait=[(SS["xa0"], SS["xa0"].v), (SS["xa1"], SS["xa1"].v),
                        (cV, xsc_guard[xp2])],
                  inc=(sxc, 16))
                xsc_ld[xp2] = sxc.v
                V("tensor_tensor", u2[:, uo:uo + CH], dtf[:, uo:uo + CH],
                  xsc[:, xp2 * CH:(xp2 + 1) * CH], ALU.mult,
                  wait=[(sxc, xsc_ld[xp2])], inc=(cV, 1))
                ms_u2 = cV.v
                zp = d % 2
                szb = SS["zb0"] if zp == 0 else SS["zb1"]
                Y("dma_start", out=zbb[:, zp * CH:(zp + 1) * CH],
                  in_=z_d[d, :, c * CH:(c + 1) * CH],
                  wait=[(SS["z0"], SS["z0"].v), (SS["z1"], SS["z1"].v),
                        (cV, zb_guard[zp])],
                  inc=(szb, 16))
                zb_ld[zp] = szb.v

                da_ms = [0] * DS
                for s in range(DS + 2):
                    if s < DS:
                        ab = s % 2
                        S("activation", da[:, ab * CH:(ab + 1) * CH],
                          dtf[:, uo:uo + CH], AF.Exp,
                          scale=A_sb[:, d * DS + s: d * DS + s + 1],
                          wait=[(cV, max(ms_dt, da_guard[ab])),
                                (cS, ms_glob_s)], inc=(cS, 1))
                        da_ms[s] = cS.v
                        sl3 = s % 3
                        V("tensor_tensor", db[:, sl3 * CH:(sl3 + 1) * CH],
                          u2[:, uo:uo + CH], Bp[:, s * CH:(s + 1) * CH],
                          ALU.mult, wait=[(cV, max(ms_u2, ms_bcp))],
                          inc=(cV, 1))
                    if 1 <= s <= DS:
                        s1_ = s - 1
                        ab = s1_ % 2
                        V("tensor_tensor_scan",
                          hbig[:, s1_ * CH:(s1_ + 1) * CH],
                          da[:, ab * CH:(ab + 1) * CH],
                          db[:, (s1_ % 3) * CH:((s1_ % 3) + 1) * CH],
                          state_sb[:, d * DS + s1_: d * DS + s1_ + 1],
                          ALU.mult, ALU.add,
                          wait=[(cS, da_ms[s1_])], inc=(cV, 1))
                        da_guard[ab] = cV.v
                    if s >= 2:
                        s2_ = s - 2
                        V("tensor_tensor", hcb[:, s2_ * CH:(s2_ + 1) * CH],
                          hbig[:, s2_ * CH:(s2_ + 1) * CH],
                          Cp[:, s2_ * CH:(s2_ + 1) * CH], ALU.mult,
                          inc=(cV, 1))
                if DEBUG and c == 1 and d == 0:
                    V(None, wait=[(cP, cP.v)])
                    Y("dma_start", out=dbg["hbig1"], in_=hbig[:],
                      wait=[(cV, cV.v)], inc=(SS["dbg"], 16))
                    Y("dma_start", out=dbg["hcb1"], in_=hcb[:],
                      wait=[(cV, cV.v)], inc=(SS["dbg"], 16))
                    V(None, wait=[(SS["dbg"], SS["dbg"].v)])
                V("tensor_copy", state_sb[:, d * DS:(d + 1) * DS],
                  hbig[:, CH - 1: DS * CH: CH], inc=(cV, 1))
                V("tensor_tensor", tr1, hcb[:, 0:8 * CH],
                  hcb[:, 8 * CH:16 * CH], ALU.add,
                  wait=[(cS, motok_sguard)], inc=(cV, 1))
                hcb_vguard = cV.v
                V("tensor_tensor", tr2, tr1[:, 0:4 * CH],
                  tr1[:, 4 * CH:8 * CH], ALU.add, wait=[(cV, cV.v)],
                  inc=(cV, 1))
                V("tensor_tensor", tr1[:, 0:2 * CH], tr2[:, 0:2 * CH],
                  tr2[:, 2 * CH:4 * CH], ALU.add, wait=[(cV, cV.v)],
                  inc=(cV, 1))
                V("tensor_tensor", ysum[:], tr1[:, 0:CH],
                  tr1[:, CH:2 * CH], ALU.add, wait=[(cV, cV.v)],
                  inc=(cV, 1))
                if DEBUG:
                    Y("dma_start", out=dbg["ssm"][d, :, c * CH:(c + 1) * CH],
                      in_=ysum[:], wait=[(cV, cV.v)], inc=(SS["dbg"], 16))
                    V(None, wait=[(SS["dbg"], SS["dbg"].v)])
                V("scalar_tensor_tensor", yt[:],
                  xsc[:, xp2 * CH:(xp2 + 1) * CH], Dv_sb[:, d:d + 1],
                  ysum[:], ALU.mult, ALU.add, wait=[(cV, cV.v)],
                  inc=(cV, 1))
                xsc_guard[xp2] = cV.v
                yp = d % 2
                syo = SS["yo0"] if yp == 0 else SS["yo1"]
                V("tensor_tensor", yob[:, yp * CH:(yp + 1) * CH], yt[:],
                  zbb[:, zp * CH:(zp + 1) * CH], ALU.mult,
                  wait=[(szb, zb_ld[zp]), (syo, yo_store[yp]),
                        (cV, cV.v)], inc=(cV, 1))
                zb_guard[zp] = cV.v
                dt_guard[wb] = cS.v
                Y("dma_start", out=yop_d[d, :, c * CH:(c + 1) * CH],
                  in_=yob[:, yp * CH:(yp + 1) * CH],
                  wait=[(cV, vfence())], inc=(syo, 16))
                yo_store[yp] = syo.v
            flush()
            psB.close()

            psC = contextlib.ExitStack()
            po = [psC.enter_context(
                nc.psum_tensor(f"po{c}_{j}", [128, 512], F32))
                for j in range(8)]
            T(None, wait=[(cV, cV.v), (cS, cS.v)])
            wkss = [SS["wk0"], SS["wk1"], SS["wk2"]]
            ykss = [SS["yk0"], SS["yk1"], SS["yk2"]]
            for half in range(2):
                wk_ld = [0, 0, 0]
                wk_guard = [0, 0, 0]
                yk_ld = [0, 0, 0]
                for k in range(3):
                    G("dma_start", out=wko[:, k * 1024:(k + 1) * 1024],
                      in_=w_out_k[k, half], wait=[(cT, wko_guard)],
                      inc=(wkss[k], 16))
                    wk_ld[k] = wkss[k].v
                    Y("dma_start", out=yks[:, k * CH:(k + 1) * CH],
                      in_=yop_d[k, :, c * CH:(c + 1) * CH],
                      wait=[(SS["yo0"], SS["yo0"].v),
                            (SS["yo1"], SS["yo1"].v), (cT, wko_guard)],
                      inc=(ykss[k], 16))
                    yk_ld[k] = ykss[k].v
                for k in range(KI):
                    sl = k % 3
                    for j in range(8):
                        w0 = ([(wkss[sl], wk_ld[sl]), (ykss[sl], yk_ld[sl])]
                              if j == 0 else ())
                        if k == 0 and j == 0 and half == 1:
                            w0 = w0 + [(cV, cV.v)]
                        T("matmul", po[j][:],
                          wko[:, sl * 1024 + j * 128:
                               sl * 1024 + (j + 1) * 128],
                          yks[:, sl * CH:(sl + 1) * CH], start=(k == 0),
                          stop=(k == KI - 1), wait=w0,
                          inc=((cT, 1) if j == 7 else None))
                    wk_guard[sl] = cT.v
                    if k + 3 < KI:
                        G("dma_start",
                          out=wko[:, sl * 1024:(sl + 1) * 1024],
                          in_=w_out_k[k + 3, half],
                          wait=[(cT, wk_guard[sl])], inc=(wkss[sl], 16))
                        wk_ld[sl] = wkss[sl].v
                        Y("dma_start", out=yks[:, sl * CH:(sl + 1) * CH],
                          in_=yop_d[k + 3, :, c * CH:(c + 1) * CH],
                          wait=[(cT, wk_guard[sl])], inc=(ykss[sl], 16))
                        yk_ld[sl] = ykss[sl].v
                ms_oph = cT.v
                wko_guard = cT.v
                for j in range(8):
                    V("tensor_copy",
                      moev[:, (half * 8 + j) * CH:(half * 8 + j + 1) * CH],
                      po[j][:], wait=[(cT, ms_oph)], inc=(cV, 1))
            ms_moev = cV.v
            flush()
            psC.close()

            psD = contextlib.ExitStack()
            ptA = psD.enter_context(nc.psum_tensor(f"ptA{c}", [128, 128], BF16))
            ptB = psD.enter_context(nc.psum_tensor(f"ptB{c}", [128, 128], BF16))
            pt2 = [ptA, ptB]
            pt2_free = [0, 0]
            nt2 = [0]
            T(None, wait=[(cV, cV.v), (cS, cS.v)])

            def transpose_p2(dst_ap, src_ap, src_ms):
                b2 = nt2[0] % 2
                T("transpose", pt2[b2][:], src_ap, ident_b[:],
                  wait=[(cV, pt2_free[b2]), (cV, src_ms)], inc=(cT, 1))
                V("tensor_copy", dst_ap, pt2[b2][:], wait=[(cT, cT.v)],
                  inc=(cV, 1))
                pt2_free[b2] = cV.v
                nt2[0] += 1

            V(None, wait=[(cS, motok_sguard)])
            for mg in range(KD):
                for tt in range(NT):
                    transpose_p2(
                        motok[:, tt * D + mg * 128: tt * D + (mg + 1) * 128],
                        moev[:, mg * CH + tt * 128: mg * CH + (tt + 1) * 128],
                        ms_moev)
            ms_motok = cV.v

            for tt in range(NT):
                ti = c * NT + tt
                Y("dma_start", out=xtok2[:],
                  in_=x_in[ti * 128:(ti + 1) * 128, :],
                  wait=[(cV, xt2_vguard)], inc=(SS["xt2"], 16))
                ms_xt2 = SS["xt2"].v
                V(None, wait=[(cV, ms_motok)])
                S(None, wait=[(cV, ms_motok), (cS, motok_sguard)])
                ln_tok(motok[:, tt * D:(tt + 1) * D], lnsq[:, 0:D], D)
                motok_sguard = cS.v
                V("tensor_tensor", x1tok[:], lnsq[:, 0:D], xtok2[:],
                  ALU.add, wait=[(SS["xt2"], ms_xt2),
                                 (SS["x1s"], SS["x1s"].v),
                                 (cS, x1_sguard), (cV, cV.v)], inc=(cV, 1))
                xt2_vguard = cV.v
                Y("dma_start", out=x1_d[ti * 128:(ti + 1) * 128, :],
                  in_=x1tok[:], wait=[(cV, vfence())], inc=(SS["x1s"], 16))
                V(None, wait=[(cV, cV.v), (cT, xn2tok_tg)])
                S(None, wait=[(cV, cV.v)])
                ln_tok(x1tok[:], xn2tok[:], D)
                x1_sguard = cS.v
                xn2tok_ms = cV.v
                V(None, wait=[(SS["x2s"], SS["x2s"].v)])
                for ct in range(KD):
                    transpose_p2(stage16[:, ct * 128:(ct + 1) * 128],
                                 xn2tok[:, ct * 128:(ct + 1) * 128],
                                 xn2tok_ms)
                xn2tok_tg = cT.v
                Y("dma_start", out=xn2_d[ti].transpose([1, 0, 2]),
                  in_=stage16.ap().rearrange("p (c t) -> p c t", t=128),
                  wait=[(cV, vfence())], inc=(SS["x2s"], 16))
            flush()
            psD.close()
        flush()
        p2.close()

        p4 = contextlib.ExitStack()
        en = p4.enter_context
        xn2 = en(nc.sbuf_tensor("xn2", [128, KD * L], BF16))
        sel8_b = en(nc.sbuf_tensor("sel8_b", [8, 8 * 128], BF16))
        wgate_sb = en(nc.sbuf_tensor("wgate_sb", [128, KD * 8], BF16))
        wsgate_sb = en(nc.sbuf_tensor("wsgate_sb", [128, KD], BF16))
        combr = en(nc.sbuf_tensor("combr", [8, L], BF16))
        wfa = en(nc.sbuf_tensor("wfa", [128, 2 * KD * 128], BF16))
        wfb = en(nc.sbuf_tensor("wfb", [128, 2 * KD * 128], BF16))
        s1b = en(nc.sbuf_tensor("s1b", [128, 2 * CH], BF16))
        gt = en(nc.sbuf_tensor("gt", [128, CH], BF16))
        hbk = en(nc.sbuf_tensor("hbk", [128, KF * L], BF16))
        wpc = en(nc.sbuf_tensor("wpc", [128, 3 * KF * 128], BF16))
        oacst = en(nc.sbuf_tensor("oacst", [128, 2 * L], F32))
        fotok = en(nc.sbuf_tensor("fotok", [128, D], BF16))
        x1t2 = en(nc.sbuf_tensor("x1t2", [128, D], F32))
        otok = en(nc.sbuf_tensor("otok", [128, D], F32))
        oacr = en(nc.sbuf_tensor("oacr", [128, 2 * KD * 128], BF16))
        pbc2 = en(nc.psum_tensor("pbcg", [128, 512], F32))

        G("dma_start", out=wgate_sb[:], in_=w_gate, inc=(SS["wg4"], 16))
        G("dma_start", out=wsgate_sb[:], in_=w_sgate, inc=(SS["wg4"], 16))
        G("dma_start", out=sel8_b[:], in_=sel8, inc=(SS["wg4"], 16))
        ms_wg = SS["wg4"].v
        for ct in range(KD):
            Y("dma_start",
              out=xn2.ap()[:, ct * L:(ct + 1) * L]
              .rearrange("p (i t) -> p i t", t=128),
              in_=xn2_d[:, ct].transpose([1, 0, 2]),
              wait=[(SS["x2s"], SS["x2s"].v)], inc=(SS["xn2l"], 16))
        ms_xn2r = SS["xn2l"].v

        p4_free = [0, 0]
        ntr4 = [0]
        cpad_tg = 0

        psG = contextlib.ExitStack()
        gbuf = psG.enter_context(nc.sbuf_tensor("gbuf", [128, L], BF16))
        psR = contextlib.ExitStack()
        lgpad = psR.enter_context(nc.sbuf_tensor("lgpad", [128, 128], BF16))
        lgt = psR.enter_context(nc.sbuf_tensor("lgt", [128, 8], F32))
        smx = psR.enter_context(nc.sbuf_tensor("smx", [128, 48], F32))
        cpad = psR.enter_context(nc.sbuf_tensor("cpad", [128, 128], BF16))
        grow = psR.enter_context(nc.sbuf_tensor("grow", [1, CH], BF16))
        pR = psR.enter_context(nc.psum_tensor("pR", [8, 512], F32))
        pG = psR.enter_context(nc.psum_tensor("pG", [1, 512], F32))
        ptrC = psR.enter_context(nc.psum_tensor("ptrC", [128, 128], BF16))
        ptrD = psR.enter_context(nc.psum_tensor("ptrD", [128, 128], BF16))
        ptr4 = [ptrC, ptrD]
        V("memset", lgpad[:], 0.0, inc=(cV, 1))
        V("memset", cpad[:], 0.0, inc=(cV, 1))
        pR_free = pG_free = pbc2_free = 0
        lgpad_guard = 0
        for c in range(NCH):
            for k in range(KD):
                w0 = ([(SS["wg4"], ms_wg), (SS["xn2l"], ms_xn2r), (cV, pR_free),
                       (cV, pG_free)] if k == 0 else ())
                T("matmul", pR[:], wgate_sb[:, k * 8:(k + 1) * 8],
                  xn2[:, k * L + c * CH: k * L + (c + 1) * CH],
                  start=(k == 0), stop=(k == KD - 1), wait=w0,
                  inc=((cT, 1) if k == KD - 1 else None))
                T("matmul", pG[:], wsgate_sb[:, k:k + 1],
                  xn2[:, k * L + c * CH: k * L + (c + 1) * CH],
                  start=(k == 0), stop=(k == KD - 1),
                  inc=((cT, 1) if k == KD - 1 else None))
            ms_rg = cT.v
            V("tensor_copy", grow[:], pG[:], wait=[(cT, ms_rg)], inc=(cV, 1))
            pG_free = cV.v
            S("activation", grow[:], grow[:], AF.Sigmoid,
              wait=[(cV, cV.v)], inc=(cS, 1))
            T("matmul", pbc2[:], ones1_b[:], grow[:], start=True, stop=True,
              wait=[(cS, cS.v), (cV, pbc2_free)], inc=(cT, 1))
            V("tensor_copy", gbuf[:, c * CH:(c + 1) * CH], pbc2[:],
              wait=[(cT, cT.v)], inc=(cV, 1))
            pbc2_free = cV.v
            for tt in range(NT):
                V("tensor_copy", lgpad[0:8, :],
                  pR[:, tt * 128:(tt + 1) * 128],
                  wait=[(cT, ms_rg), (cT, lgpad_guard), (cV, ms_glob_v)],
                  inc=(cV, 1))
                lg_ms = cV.v
                b = ntr4[0] % 2
                T("transpose", ptr4[b][:], lgpad[:],
                  ident_b[:], wait=[(cV, p4_free[b]), (cV, lg_ms)],
                  inc=(cT, 1))
                lgpad_guard = cT.v
                V("tensor_copy", lgt[:], ptr4[b][:, 0:8],
                  wait=[(cT, cT.v)], inc=(cV, 1))
                p4_free[b] = cV.v
                ntr4[0] += 1
                V("reduce_max", smx[:, 0:1], lgt[:], axis=AX.X, negate=True,
                  wait=[(cV, cV.v)], inc=(cV, 1))
                S("activation", smx[:, 8:16], lgt[:], AF.Exp,
                  bias=smx[:, 0:1], wait=[(cV, cV.v)], inc=(cS, 1))
                V("reduce_sum", smx[:, 1:2], smx[:, 8:16], axis=AX.X,
                  wait=[(cS, cS.v)], inc=(cV, 1))
                V("reciprocal", smx[:, 2:3], smx[:, 1:2],
                  wait=[(cV, cV.v)], inc=(cV, 1))
                V("tensor_scalar", smx[:, 16:24], smx[:, 8:16],
                  smx[:, 2:3], None, ALU.mult, wait=[(cV, cV.v)],
                  inc=(cV, 1))
                V("reduce_max", smx[:, 3:4], smx[:, 16:24], axis=AX.X,
                  wait=[(cV, cV.v)], inc=(cV, 1))
                V("tensor_scalar", smx[:, 24:32], smx[:, 16:24],
                  smx[:, 3:4], None, ALU.is_equal, wait=[(cV, cV.v)],
                  inc=(cV, 1))
                V("scalar_tensor_tensor", smx[:, 32:40], smx[:, 24:32],
                  -2.0, smx[:, 16:24], ALU.mult, ALU.add,
                  wait=[(cV, cV.v)], inc=(cV, 1))
                V("reduce_max", smx[:, 4:5], smx[:, 32:40], axis=AX.X,
                  wait=[(cV, cV.v)], inc=(cV, 1))
                V("tensor_scalar", smx[:, 40:48], smx[:, 16:24],
                  smx[:, 4:5], None, ALU.is_ge, wait=[(cV, cV.v)],
                  inc=(cV, 1))
                V("tensor_tensor", cpad[:, 0:8], smx[:, 16:24],
                  smx[:, 40:48], ALU.mult,
                  wait=[(cV, cV.v), (cT, cpad_tg)], inc=(cV, 1))
                b = ntr4[0] % 2
                T("transpose", ptr4[b][:], cpad[:], ident_b[:],
                  wait=[(cV, p4_free[b]), (cV, cV.v)], inc=(cT, 1))
                cpad_tg = cT.v
                V("tensor_copy",
                  combr[:, c * CH + tt * 128: c * CH + (tt + 1) * 128],
                  ptr4[b][0:8, :], wait=[(cT, cT.v)], inc=(cV, 1))
                p4_free[b] = cV.v
                ntr4[0] += 1
            pR_free = cV.v
        flush()
        psR.close()
        psM = contextlib.ExitStack()
        pH1s = [psM.enter_context(nc.psum_tensor("pH1a", [128, 512], F32)),
                psM.enter_context(nc.psum_tensor("pH1b", [128, 512], F32))]
        pH2s = [psM.enter_context(nc.psum_tensor("pH2a", [128, 512], F32)),
                psM.enter_context(nc.psum_tensor("pH2b", [128, 512], F32))]
        pEs = [psM.enter_context(nc.psum_tensor("pEa", [128, 512], F32)),
               psM.enter_context(nc.psum_tensor("pEb", [128, 512], F32))]
        ms_comb = cV.v
        ms_gbuf = cV.v
        if DEBUG:
            Y("dma_start", out=dbg["comb"], in_=combr[:],
              wait=[(cV, ms_comb)], inc=(SS["dbg"], 16))

        wf_free = [0, 0]
        s1_guard = [0, 0]
        hst_store = [0, 0]
        pH1_sfree = [0, 0]
        pH2_free = [0, 0]
        pE_free = [0, 0]
        gt_guard = 0
        ms_hsh = {}
        hst = hbk.ap()[:, 0:2 * L]
        wfss = [SS["wf0"], SS["wf1"]]
        hsss = [SS["hs0"], SS["hs1"]]
        for m in range(KS):
            b = m % 2
            G("dma_start", out=wfa[:, b * KD * 128:(b + 1) * KD * 128],
              in_=w_s1[m], wait=[(cT, wf_free[b])], inc=(wfss[b], 16))
            G("dma_start", out=wfb[:, b * KD * 128:(b + 1) * KD * 128],
              in_=w_s2[m], inc=(wfss[b], 16))
            ms_wf = wfss[b].v
            hp = m % 2
            for c in range(NCH):
                cp = c % 2
                for k in range(KD):
                    w0 = ([(wfss[b], ms_wf), (SS["xn2l"], ms_xn2r),
                           (cS, pH1_sfree[cp]),
                           (cV, pH2_free[cp])] if k == 0 else ())
                    T("matmul", pH1s[cp][:],
                      wfa[:, b * KD * 128 + k * 128:
                          b * KD * 128 + (k + 1) * 128],
                      xn2[:, k * L + c * CH: k * L + (c + 1) * CH],
                      start=(k == 0), stop=(k == KD - 1), wait=w0,
                      inc=((cT, 1) if k == KD - 1 else None))
                    T("matmul", pH2s[cp][:],
                      wfb[:, b * KD * 128 + k * 128:
                          b * KD * 128 + (k + 1) * 128],
                      xn2[:, k * L + c * CH: k * L + (c + 1) * CH],
                      start=(k == 0), stop=(k == KD - 1),
                      inc=((cT, 1) if k == KD - 1 else None))
                ms_h12 = cT.v
                spp = c % 2
                S("activation", s1b[:, spp * CH:(spp + 1) * CH], pH1s[cp][:],
                  AF.Silu, wait=[(cT, ms_h12), (cV, s1_guard[spp])],
                  inc=(cS, 1))
                pH1_sfree[cp] = cS.v
                V("tensor_tensor", gt[:], s1b[:, spp * CH:(spp + 1) * CH],
                  pH2s[cp][:], ALU.mult, wait=[(cS, cS.v), (cV, gt_guard)],
                  inc=(cV, 1))
                s1_guard[spp] = cV.v
                pH2_free[cp] = cV.v
                V("tensor_tensor",
                  hst[:, hp * L + c * CH: hp * L + (c + 1) * CH],
                  gt[:], gbuf[:, c * CH:(c + 1) * CH], ALU.mult,
                  wait=[(cV, ms_gbuf), (hsss[hp], hst_store[hp]),
                        (cV, cV.v)], inc=(cV, 1))
                gt_guard = cV.v
            wf_free[b] = cT.v
            Y("dma_start", out=hsh_d[m], in_=hst[:, hp * L:(hp + 1) * L],
              wait=[(cV, vfence())], inc=(hsss[hp], 16))
            hst_store[hp] = hsss[hp].v

        flush()
        psG.close()
        cbe = en(nc.sbuf_tensor("cbe", [128, L], BF16))
        wpc_free = [0, 0, 0]
        wpss = [SS["wp0"], SS["wp1"]]
        oass = [SS["oa0"], SS["oa1"], SS["oa2"], SS["oa3"]]
        oacst_guard = [(oass[0], 0)] * 2
        oacst_rr = [0]
        hbk_tguard = 0
        for kg in range(4):
            Y("dma_start",
              out=hbk.ap().rearrange("p (a l) -> p a l", l=L),
              in_=hsh_d[kg * KF:(kg + 1) * KF].transpose([1, 0, 2]),
              wait=[(SS["hs0"], SS["hs0"].v), (SS["hs1"], SS["hs1"].v),
                    (cT, hbk_tguard)],
              inc=(SS["hb"], 16))
            ms_hbk = SS["hb"].v
            wpc_ld = {}
            for mpre in range(2):
                b = mpre % 3
                G("dma_start",
                  out=wpc[:, b * KF * 128:(b + 1) * KF * 128],
                  in_=w_sp[mpre][:, kg * KF * 128:(kg + 1) * KF * 128],
                  wait=[(cT, wpc_free[b])], inc=(wpss[b % 2], 16))
                wpc_ld[mpre] = wpss[b % 2].v
            for m in range(KD):
                b = m % 3
                if m + 2 < KD:
                    b2 = (m + 2) % 3
                    G("dma_start",
                      out=wpc[:, b2 * KF * 128:(b2 + 1) * KF * 128],
                      in_=w_sp[m + 2][:, kg * KF * 128:(kg + 1) * KF * 128],
                      wait=[(cT, wpc_free[b2])], inc=(wpss[b2 % 2], 16))
                    wpc_ld[m + 2] = wpss[b2 % 2].v
                ms_wpc = wpc_ld[m]
                for c in range(NCH):
                    cp = c % 2
                    for k in range(KF):
                        w0 = ([(wpss[b % 2], ms_wpc), (SS["hb"], ms_hbk),
                               (cV, pE_free[cp])] if k == 0 else ())
                        T("matmul", pEs[cp][:],
                          wpc[:, b * KF * 128 + k * 128:
                              b * KF * 128 + (k + 1) * 128],
                          hbk[:, k * L + c * CH: k * L + (c + 1) * CH],
                          start=(k == 0), stop=(k == KF - 1), wait=w0,
                          inc=((cT, 1) if k == KF - 1 else None))
                    ms_pe = cT.v
                    op = m % 2
                    gsem, gval = oacst_guard[op]
                    V("tensor_copy", oacst[:, op * L + c * CH:
                                           op * L + (c + 1) * CH],
                      pEs[cp][:], wait=[(cT, ms_pe), (gsem, gval)],
                      inc=(cV, 1))
                    pE_free[cp] = cV.v
                grp = oass[m % 4]
                G("dma_start", out=oac_d[m],
                  in_=oacst[:, op * L:(op + 1) * L],
                  wait=[(cV, vfence()), (grp, grp.v)],
                  inc=(grp, 16),
                  **({} if kg == 0 else {"accum_op": ALU.add}))
                oacst_guard[op] = (grp, grp.v)
                wpc_free[b] = cT.v
            hbk_tguard = cT.v

        he_tguard = cT.v
        for e in range(E):
            for c in range(NCH):
                T("matmul", pbc2[:], sel8_b[:, e * 128:(e + 1) * 128],
                  combr[:, c * CH:(c + 1) * CH], start=True, stop=True,
                  wait=[(cV, max(ms_comb, pbc2_free)), (SS["wg4"], ms_wg)],
                  inc=(cT, 1))
                V("tensor_copy", cbe[:, c * CH:(c + 1) * CH], pbc2[:],
                  wait=[(cT, cT.v)], inc=(cV, 1))
                pbc2_free = cV.v
            ms_cbe = cV.v
            for m in range(KF):
                b = m % 2
                G("dma_start", out=wfa[:, b * KD * 128:(b + 1) * KD * 128],
                  in_=w_f1[e, m], wait=[(cT, wf_free[b])],
                  inc=(wfss[b], 16))
                G("dma_start", out=wfb[:, b * KD * 128:(b + 1) * KD * 128],
                  in_=w_f2[e, m], inc=(wfss[b], 16))
                ms_wf = wfss[b].v
                for c in range(NCH):
                    cp = c % 2
                    for k in range(KD):
                        w0 = ([(wfss[b], ms_wf), (SS["xn2l"], ms_xn2r),
                               (cS, pH1_sfree[cp]),
                               (cV, pH2_free[cp])] if k == 0 else ())
                        T("matmul", pH1s[cp][:],
                          wfa[:, b * KD * 128 + k * 128:
                              b * KD * 128 + (k + 1) * 128],
                          xn2[:, k * L + c * CH: k * L + (c + 1) * CH],
                          start=(k == 0), stop=(k == KD - 1), wait=w0,
                          inc=((cT, 1) if k == KD - 1 else None))
                        T("matmul", pH2s[cp][:],
                          wfb[:, b * KD * 128 + k * 128:
                              b * KD * 128 + (k + 1) * 128],
                          xn2[:, k * L + c * CH: k * L + (c + 1) * CH],
                          start=(k == 0), stop=(k == KD - 1),
                          inc=((cT, 1) if k == KD - 1 else None))
                    ms_h12 = cT.v
                    spp = c % 2
                    S("activation", s1b[:, spp * CH:(spp + 1) * CH], pH1s[cp][:],
                      AF.Silu, wait=[(cT, ms_h12), (cV, s1_guard[spp])],
                      inc=(cS, 1))
                    pH1_sfree[cp] = cS.v
                    V("tensor_tensor", gt[:],
                      s1b[:, spp * CH:(spp + 1) * CH], pH2s[cp][:], ALU.mult,
                      wait=[(cS, cS.v), (cV, gt_guard)], inc=(cV, 1))
                    s1_guard[spp] = cV.v
                    pH2_free[cp] = cV.v
                    V("tensor_tensor",
                      hbk[:, m * L + c * CH: m * L + (c + 1) * CH],
                      gt[:], cbe[:, c * CH:(c + 1) * CH], ALU.mult,
                      wait=[(cV, ms_cbe), (cT, he_tguard), (cV, cV.v)],
                      inc=(cV, 1))
                    gt_guard = cV.v
                wf_free[b] = cT.v
            ms_he = cV.v
            wpc_ld = {}
            for mpre in range(2):
                b = mpre % 3
                G("dma_start",
                  out=wpc[:, b * KF * 128:(b + 1) * KF * 128],
                  in_=w_ep[e, mpre], wait=[(cT, wpc_free[b])],
                  inc=(wpss[b % 2], 16))
                wpc_ld[mpre] = wpss[b % 2].v
            for m2 in range(KD):
                b = m2 % 3
                if m2 + 2 < KD:
                    b2 = (m2 + 2) % 3
                    G("dma_start",
                      out=wpc[:, b2 * KF * 128:(b2 + 1) * KF * 128],
                      in_=w_ep[e, m2 + 2], wait=[(cT, wpc_free[b2])],
                      inc=(wpss[b2 % 2], 16))
                    wpc_ld[m2 + 2] = wpss[b2 % 2].v
                ms_wpc = wpc_ld[m2]
                for c in range(NCH):
                    cp = c % 2
                    for k in range(KF):
                        w0 = ([(wpss[b % 2], ms_wpc), (cV, ms_he),
                               (cV, pE_free[cp])] if k == 0 else ())
                        T("matmul", pEs[cp][:],
                          wpc[:, b * KF * 128 + k * 128:
                              b * KF * 128 + (k + 1) * 128],
                          hbk[:, k * L + c * CH: k * L + (c + 1) * CH],
                          start=(k == 0), stop=(k == KF - 1), wait=w0,
                          inc=((cT, 1) if k == KF - 1 else None))
                    ms_pe = cT.v
                    op = m2 % 2
                    gsem, gval = oacst_guard[op]
                    V("tensor_copy", oacst[:, op * L + c * CH:
                                           op * L + (c + 1) * CH],
                      pEs[cp][:], wait=[(cT, ms_pe), (gsem, gval)],
                      inc=(cV, 1))
                    pE_free[cp] = cV.v
                grp = oass[m2 % 4]
                G("dma_start", out=oac_d[m2],
                  in_=oacst[:, op * L:(op + 1) * L],
                  wait=[(cV, vfence()), (grp, grp.v)],
                  inc=(grp, 16), accum_op=ALU.add)
                oacst_guard[op] = (grp, grp.v)
                wpc_free[b] = cT.v
            he_tguard = cT.v

        fo_sguard = 0
        x1t2_vg = 0
        flush()
        psM.close()
        psF = contextlib.ExitStack()
        ptrE = psF.enter_context(nc.psum_tensor("ptrE", [128, 128], BF16))
        ptrF = psF.enter_context(nc.psum_tensor("ptrF", [128, 128], BF16))
        ptr4 = [ptrE, ptrF]
        p4_free = [0, 0]
        oa_full = [(s, s.v) for s in oass]
        oacr_tg = [0, 0]
        oacr_ld = [0, 0]
        for ti in range(L // 128 + 1):
            if ti < L // 128:
                ob = ti % 2
                G("dma_start",
                  out=oacr.ap()[:, ob * KD * 128:(ob + 1) * KD * 128]
                  .rearrange("p (c t) -> p c t", t=128),
                  in_=oac_d[:, :, ti * 128:(ti + 1) * 128]
                  .transpose([1, 0, 2]),
                  wait=oa_full + [(cT, oacr_tg[ob])], inc=(SS["oar"], 16))
                oacr_ld[ob] = SS["oar"].v
            if ti == 0:
                continue
            tj = ti - 1
            ob = tj % 2
            Y("dma_start", out=x1t2[:],
              in_=x1_d[tj * 128:(tj + 1) * 128, :],
              wait=[(SS["x1s"], SS["x1s"].v), (cV, x1t2_vg)],
              inc=(SS["x12"], 16))
            ms_x1l = SS["x12"].v
            for ct in range(KD):
                b = ntr4[0] % 2
                T("transpose", ptr4[b][:],
                  oacr[:, ob * KD * 128 + ct * 128:
                       ob * KD * 128 + (ct + 1) * 128],
                  ident_b[:],
                  wait=[(cV, p4_free[b]), (SS["oar"], oacr_ld[ob])],
                  inc=(cT, 1))
                V("tensor_copy", fotok[:, ct * 128:(ct + 1) * 128],
                  ptr4[b][:], wait=[(cT, cT.v), (cS, fo_sguard)],
                  inc=(cV, 1))
                p4_free[b] = cV.v
                ntr4[0] += 1
            oacr_tg[ob] = cT.v
            V(None, wait=[(cV, cV.v)])
            S(None, wait=[(cV, cV.v), (cS, fo_sguard)])
            ln_tok(fotok[:], lnsq[:, 0:D], D)
            fo_sguard = cS.v
            V("tensor_tensor", otok[:], lnsq[:, 0:D], x1t2[:], ALU.add,
              wait=[(SS["x12"], ms_x1l), (SS["outs"], SS["outs"].v),
                    (cV, cV.v)], inc=(cV, 1))
            x1t2_vg = cV.v
            Y("dma_start", out=out[tj * 128:(tj + 1) * 128, :],
              in_=otok[:], wait=[(cV, vfence())], inc=(SS["outs"], 16))
        flush()
        psF.close()
        p4.close()
        Y(None, wait=[(s, s.v) for s in SS.values()] + [(sq, sq.v),
                                                        (gq, gq.v)])
        flush()

    return nc


def prep_shared(inputs):
    f = {}
    f["w_in"] = tile_lhsT(inputs["L0_attn_in_proj_w"], KD, 2 * KI)
    xp = np.asarray(inputs["L0_attn_x_proj_w"], np.float32)
    f["w_xdt"] = tile_lhsT(xp[0:128], KI, 1)[0]
    f["w_xB"] = tile_lhsT(xp[128:144], KI, 1, msz=16)[0]
    f["w_xC"] = tile_lhsT(xp[144:160], KI, 1, msz=16)[0]
    f["w_dtp"] = tile_lhsT(inputs["L0_attn_dt_proj_w"], 1, KI)
    wo = tile_lhsT(inputs["L0_attn_out_proj_w"], KI, KD)
    wo = wo.reshape(2, 8, 128, KI, 128).transpose(3, 0, 2, 1, 4)
    f["w_out_k"] = _t(wo.reshape(KI, 2, 128, 8 * 128))
    f["w_gate"] = tile_lhsT(inputs["L0_ffn_gate_w"], KD, 1, msz=8)[0]
    f["w_sgate"] = tile_lhsT(inputs["L0_ffn_shared_gate_w"], KD, 1, msz=1)[0]
    f["w_f1"] = np.stack([tile_lhsT(inputs["L0_ffn_expert_fc1"][e], KD, KF)
                          for e in range(E)])
    f["w_f2"] = np.stack([tile_lhsT(inputs["L0_ffn_expert_fc2"][e], KD, KF)
                          for e in range(E)])
    f["w_ep"] = np.stack([tile_lhsT(inputs["L0_ffn_expert_proj"][e], KF, KD)
                          for e in range(E)])
    f["w_s1"] = tile_lhsT(inputs["L0_ffn_shared_fc1"], KD, KS)
    f["w_s2"] = tile_lhsT(inputs["L0_ffn_shared_fc2"], KD, KS)
    f["w_sp"] = tile_lhsT(inputs["L0_ffn_shared_proj"], KS, KD)
    cw = np.asarray(inputs["L0_attn_conv_w"], np.float32)
    f["c_w"] = _t(cw.reshape(KI, 128, 4).transpose(1, 0, 2)
                  .reshape(128, KI * 4))
    f["c_b"] = col128(inputs["L0_attn_conv_b"], KI)
    f["dt_b"] = col128(inputs["L0_attn_dt_proj_b"], KI)
    al = np.asarray(inputs["L0_attn_A_log"], np.float32)
    f["a_log"] = _t(al.reshape(KI, 128, DS).transpose(1, 0, 2)
                    .reshape(128, KI * DS))
    f["d_vec"] = col128(inputs["L0_attn_D"], KI)
    f["ident"] = np.eye(128, dtype=ml_dtypes.bfloat16)
    s16 = np.zeros((16, 16 * 128), dtype=ml_dtypes.bfloat16)
    for s in range(16):
        s16[s, s * 128:(s + 1) * 128] = 1
    f["sel16"] = s16
    s8 = np.zeros((8, 8 * 128), dtype=ml_dtypes.bfloat16)
    for s in range(8):
        s8[s, s * 128:(s + 1) * 128] = 1
    f["sel8"] = s8
    f["ones1"] = np.ones((1, 128), dtype=ml_dtypes.bfloat16)
    return f


def kernel(**inputs):
    x = np.asarray(inputs["L0_attn_hidden_states"], np.float32)
    shared = prep_shared(inputs)
    nc = build_nc()
    in_maps = [dict(shared, x=_t(x[b])) for b in range(8)]
    res = run_bass_kernel_spmd(nc, in_maps, core_ids=list(range(8)))
    outs = np.stack([np.asarray(res.results[b]["out"], np.float32)
                     for b in range(8)], axis=0)
    return outs
